# version 05
# baseline (speedup 1.0000x reference)
import numpy as np
import ml_dtypes
from contextlib import ExitStack

import concourse.bass as bass
import concourse.mybir as mybir
from concourse.bass_utils import run_bass_kernel_spmd

F32 = mybir.dt.float32
BF16 = mybir.dt.bfloat16
AF = mybir.ActivationFunctionType
ALU = mybir.AluOpType
AX = mybir.AxisListType

D = 2048
S = 2048
H = 16
DM = 64
HID = 5632
NDC = D // 128
NCT = S // 512
NHC = HID // 128
TB = 512
NB = S // TB
LN_EPS = 1e-5
ATTN_EPS = 1e-6
FFN_HALF_EPS = 0.5e-8
NW = float(HID * D)

DEBUG = None


class Sem:
    def __init__(self, h):
        self.h = h
        self.n = 0

    def bump(self, k=1):
        self.n += k
        return self.n


class B:
    ENGS = ("t", "v", "s", "g", "y")

    def __init__(self, nc):
        self.nc = nc
        self.ops = {k: [] for k in self.ENGS}
        self.waited = {k: {} for k in self.ENGS}
        self.all_sems = []

    def wait(self, eng, deps):
        if deps is None:
            return
        if isinstance(deps, tuple):
            deps = [deps]
        flat = []
        for d in deps:
            if isinstance(d, list):
                flat.extend(d)
            else:
                flat.append(d)
        for d in flat:
            if d is None:
                continue
            sem, t = d
            if self.waited[eng].get(id(sem), 0) >= t:
                continue
            self.waited[eng][id(sem)] = t
            self.ops[eng].append(lambda e, s=sem, t=t: e.wait_ge(s.h, t))

    def op(self, eng, fn, deps=None, inc=None, inck=1):
        self.wait(eng, deps)
        if inc is not None:
            tgt = inc.bump(inck)
            self.ops[eng].append(
                lambda e, fn=fn, s=inc, k=inck: fn(e).then_inc(s.h, k))
            return (inc, tgt)
        self.ops[eng].append(fn)
        return None

    def gate(self, engs=None):
        deps = [(s, s.n) for s in self.all_sems if s.n > 0]
        for eng in (engs or self.ENGS):
            self.wait(eng, deps)


class BankMgr:
    def __init__(self, b, tensors):
        self.b = b
        self.t = tensors
        self.last = [None] * 8

    def use(self, idx, eng="t"):
        self.b.wait(eng, self.last[idx])
        return self.t[idx]

    def done(self, idx, dep):
        self.last[idx] = dep


def build_nc(debug=None):
    nc = bass.Bass("TRN2", target_bir_lowering=False)
    dt = nc.dram_tensor

    xT = dt("xT", [D, S], F32, kind="ExternalInput").ap()
    wq_t = dt("wq_t", [H, 128, D], F32, kind="ExternalInput").ap()
    wk_t = dt("wk_t", [H, 128, D], F32, kind="ExternalInput").ap()
    wv_t = dt("wv_t", [H, 128, D], F32, kind="ExternalInput").ap()
    wo_t = dt("wo_t", [H, 128, D], F32, kind="ExternalInput").ap()
    wmq_t = dt("wmq_t", [128, DM], F32, kind="ExternalInput").ap()
    wmk_t = dt("wmk_t", [128, DM], F32, kind="ExternalInput").ap()
    bias_t = dt("bias_t", [128, 4 * H], F32, kind="ExternalInput").ap()
    w1_t = dt("w1_t", [NHC, 128, D], F32, kind="ExternalInput").ap()
    w2_t = dt("w2_t", [NHC, 128, D], F32, kind="ExternalInput").ap()
    w3_t = dt("w3_t", [4, NHC, 128, 512], F32, kind="ExternalInput").ap()
    ones_bf = dt("ones_bf", [128, 128], BF16, kind="ExternalInput").ap()
    ones_f = dt("ones_f", [128, 128], F32, kind="ExternalInput").ap()
    ident_f = dt("ident_f", [128, 128], F32, kind="ExternalInput").ap()
    cvals = dt("cvals", [128, 8], F32, kind="ExternalInput").ap()

    out_t = dt("out_t", [NDC, 128, S], F32, kind="ExternalOutput").ap()
    x2_scr = dt("x2_scr", [NDC, 128, S], F32, kind="Internal").ap()
    y_scr = dt("y_scr", [NDC, 128, S], F32, kind="Internal").ap()
    qw1_scr = dt("qw1_scr", [NHC, 128, D], BF16, kind="Internal").ap()
    qw2_scr = dt("qw2_scr", [NHC, 128, D], BF16, kind="Internal").ap()
    qw3_scr = dt("qw3_scr", [4, 11, 128, 2048], BF16, kind="Internal").ap()

    dbg = None
    if debug is not None:
        dbg = dt("dbg", [64, 128, 2048], F32, kind="ExternalOutput").ap()

    b = B(nc)
    top = ExitStack()

    def mksem(name):
        s = Sem(top.enter_context(nc.semaphore(name)))
        b.all_sems.append(s)
        return s

    s_t = mksem("s_t")
    s_v = mksem("s_v")
    s_a = mksem("s_a")
    s_gw = mksem("s_gw")
    s_sl = mksem("s_sl")
    s_st = mksem("s_st")
    s_p = mksem("s_p")

    def sbuf(stk, name, shape, dtype):
        return stk.enter_context(nc.sbuf_tensor(name, shape, dtype))

    ones_b = sbuf(top, "ones_b", [128, 128], BF16)
    ones_s = sbuf(top, "ones_s", [128, 128], F32)
    ident_s = sbuf(top, "ident_s", [128, 128], F32)
    cval_s = sbuf(top, "cval_s", [128, 8], F32)
    bias_s = sbuf(top, "bias_s", [128, 4 * H], F32)
    wmq_s = sbuf(top, "wmq_s", [128, DM], BF16)
    wmk_s = sbuf(top, "wmk_s", [128, DM], BF16)
    buf1 = sbuf(top, "buf1", [128, NDC, 2048], BF16)
    gacc = sbuf(top, "gacc", [128, 264], F32)
    thrs = sbuf(top, "thrs", [128, 6], F32)

    pb = [top.enter_context(nc.psum_tensor(f"pb{i}", [128, 512], F32))
          for i in range(8)]
    banks = BankMgr(b, pb)

    dbg_n = {"i": 0}

    def dump(src_fn, nchunk, width, deps):
        if dbg is None:
            return
        for c in range(nchunk):
            i = dbg_n["i"]
            dbg_n["i"] += 1
            b.op("g", lambda e, c=c, i=i, w=width:
                 e.dma_start(out=dbg[i, :, 0:w], in_=src_fn(c)),
                 deps=deps, inc=s_st, inck=16)

    for dst, src in ((ones_b, ones_bf), (ones_s, ones_f), (ident_s, ident_f),
                     (cval_s, cvals), (bias_s, bias_t)):
        b.op("y", lambda e, d=dst, s=src: e.dma_start(out=d[:], in_=s),
             inc=s_sl, inck=16)
    consts_dep = (s_sl, s_sl.n)
    for dst, src in ((wmq_s, wmq_t), (wmk_s, wmk_t)):
        b.op("g", lambda e, d=dst, s=src: e.dma_start(out=d[:], in_=s),
             inc=s_gw, inck=16)
    wmaps_dep = (s_gw, s_gw.n)

    rows_guard = []

    def stats_pass(src, sq_scr, chunk_dep, nchunk=NDC, width=2048):
        nct = width // 512
        sq_guard = [None, None]
        mm_dep = None
        for dc in range(nchunk):
            j = dc % 2
            sqd = b.op("s", lambda e, dc=dc, j=j, w=width:
                       e.activation(out=sq_scr[j][:, 0:w], in_=src(dc),
                                    func=AF.Square),
                       deps=[chunk_dep(dc), sq_guard[j]], inc=s_a)
            for ct in range(nct):
                bt = banks.use(ct) if dc == 0 else pb[ct]
                b.op("t", lambda e, dc=dc, ct=ct, bt=bt:
                     e.matmul(bt[0:1, :], ones_b[:, 0:1],
                              src(dc)[:, ct * 512:(ct + 1) * 512],
                              start=(dc == 0), stop=(dc == nchunk - 1)),
                     deps=[sqd] if ct == 0 else None)
            for ct in range(nct):
                bt = banks.use(4 + ct) if dc == 0 else pb[4 + ct]
                mm_dep = b.op("t", lambda e, dc=dc, ct=ct, j=j, bt=bt:
                              e.matmul(bt[0:1, :], ones_b[:, 0:1],
                                       sq_scr[j][:, ct * 512:(ct + 1) * 512],
                                       start=(dc == 0), stop=(dc == nchunk - 1)),
                              inc=s_t if ct == nct - 1 else None)
            sq_guard[j] = mm_dep
        return mm_dep

    def ln_finish(stats_dep, T1, T2, SCR, width=2048, denom=2048.0,
                  xb=None, yb=None):
        nct = width // 512
        xb = xb or list(range(nct))
        yb = yb or list(range(4, 4 + nct))
        evs = []
        for ct in range(nct):
            evs.append(b.op("v", lambda e, ct=ct, T1=T1, bi=xb[ct]:
                            e.tensor_copy(T1[0:1, ct * 512:(ct + 1) * 512],
                                          pb[bi][0:1, :]),
                            deps=[stats_dep] + rows_guard, inc=s_v))
            evs.append(b.op("v", lambda e, ct=ct, T2=T2, bi=yb[ct]:
                            e.tensor_copy(T2[0:1, ct * 512:(ct + 1) * 512],
                                          pb[bi][0:1, :]),
                            deps=[stats_dep], inc=s_v))
        dep = evs[-1]
        bc_x = []
        bc_xx = []
        for ct in range(nct):
            b.wait("t", [dep])
            g1 = b.op("t", lambda e, ct=ct, T1=T1, bi=xb[ct]:
                      e.matmul(pb[bi][:], ones_s[0:1, :],
                               T1[0:1, ct * 512:(ct + 1) * 512],
                               start=True, stop=True),
                      inc=s_t)
            bc_x.append(g1)
            g2 = b.op("t", lambda e, ct=ct, T2=T2, bi=yb[ct]:
                      e.matmul(pb[bi][:], ones_s[0:1, :],
                               T2[0:1, ct * 512:(ct + 1) * 512],
                               start=True, stop=True),
                      inc=s_t)
            bc_xx.append(g2)
        s4 = None
        for ct in range(nct):
            s4 = b.op("s", lambda e, ct=ct, SCR=SCR, bi=xb[ct]:
                      e.activation(out=SCR[:, ct * 512:(ct + 1) * 512],
                                   in_=pb[bi][:], func=AF.Copy,
                                   scale=1.0 / denom),
                      deps=[bc_x[ct]], inc=s_a)
            banks.done(xb[ct], s4)
        s5 = b.op("s", lambda e, T1=T1, SCR=SCR, w=width:
                  e.activation(out=T1[:, 0:w], in_=SCR[:, 0:w],
                               func=AF.Square),
                  deps=[bc_x[-1], bc_xx[-1]], inc=s_a)
        s6 = None
        for ct in range(nct):
            s6 = b.op("v", lambda e, ct=ct, T1=T1, T2=T2, bi=yb[ct]:
                      e.scalar_tensor_tensor(
                          T2[:, ct * 512:(ct + 1) * 512], pb[bi][:],
                          1.0 / denom, T1[:, ct * 512:(ct + 1) * 512],
                          ALU.mult, ALU.subtract),
                      deps=[s5], inc=s_v)
            banks.done(yb[ct], s6)
        b.op("s", lambda e, T1=T1, T2=T2, w=width:
             e.activation(out=T1[:, 0:w], in_=T2[:, 0:w], func=AF.Ln,
                          bias=cval_s[:, 0:1]),
             deps=[s6, consts_dep])
        s8 = b.op("s", lambda e, T1=T1, T2=T2, w=width:
                  e.activation(out=T2[:, 0:w], in_=T1[:, 0:w], func=AF.Exp,
                               scale=-0.5), inc=s_a)
        s9 = b.op("v", lambda e, T1=T1, T2=T2, SCR=SCR, w=width:
                  e.tensor_mul(T1[:, 0:w], SCR[:, 0:w], T2[:, 0:w]),
                  deps=[s8], inc=s_v)
        rows_guard.clear()
        rows_guard.extend([s9])
        return T2[:], T1[:], s9

    stA = ExitStack()
    sqA = [sbuf(stA, f"sqA{i}", [128, 2048], BF16) for i in range(2)]
    T1A = sbuf(stA, "T1A", [128, 2048], F32)
    T2A = sbuf(stA, "T2A", [128, 2048], F32)
    SCRA = sbuf(stA, "SCRA", [128, 2048], F32)
    tmp_bf = sbuf(stA, "tmp_bf", [128, 2048], BF16)

    xb_cast = [None] * NDC
    for dc in range(NDC):
        xb_cast[dc] = b.op("g", lambda e, dc=dc:
                           e.dma_start(out=buf1[:, dc, :],
                                       in_=xT[dc * 128:(dc + 1) * 128, :]),
                           inc=s_gw, inck=16)

    a_stats = stats_pass(lambda dc: buf1[:, dc, :], sqA,
                         lambda dc: xb_cast[dc])
    rstdA, mrA, a_bcast = ln_finish(a_stats, T1A, T2A, SCRA)

    xn1_done = None
    for dc in range(NDC):
        b.op("v", lambda e, dc=dc: e.tensor_mul(tmp_bf[:], buf1[:, dc, :],
                                                rstdA),
             deps=[a_bcast])
        xn1_done = b.op("v", lambda e, dc=dc:
                        e.tensor_sub(buf1[:, dc, :], tmp_bf[:], mrA),
                        inc=s_v)

    if debug == "xn1":
        dump(lambda c: buf1[:, c, :], NDC, 2048, [xn1_done])

    stA.close()
    b.gate()

    stQ = ExitStack()
    q2_t = sbuf(stQ, "q2_t", [128, 8, 2048], BF16)
    kv_sb = sbuf(stQ, "kv_sb", [128, 8, 258], BF16)
    sc32_t = sbuf(stQ, "sc32", [128, 16384], BF16)
    k2T = sc32_t[:].rearrange("p (s q) -> p s q", s=16)
    q2s = sc32_t[:].rearrange("p (j q) -> p j q", j=8)

    stGB = ExitStack()
    gbuf = [sbuf(stGB, f"gbuf{i}", [128, 1024], F32) for i in range(2)]
    gamma_srcs = []
    for w_ap in (w1_t, w2_t):
        for hc in range(NHC):
            for half in range(2):
                gamma_srcs.append(w_ap[hc, :, half * 1024:(half + 1) * 1024])
    for g4 in range(4):
        for hc in range(0, NHC, 2):
            gamma_srcs.append(w3_t[g4, hc:hc + 2].rearrange("h p c -> p h c"))
    NG = len(gamma_srcs)
    g_state = {"dma": 0, "act": 0}
    g_dma_tgt = [None] * NG
    g_act_tgt = [None] * NG

    def pump_gamma_dma(n):
        while g_state["dma"] < min(g_state["act"] + 2, NG) and n > 0:
            k = g_state["dma"]
            dep = g_act_tgt[k - 2] if k >= 2 else None
            src_ap = gamma_srcs[k]
            dst_ap = gbuf[k % 2][:]
            if len(src_ap.shape) == 3:
                dst_ap = dst_ap.rearrange("p (h c) -> p h c", h=src_ap.shape[1])
            g_dma_tgt[k] = b.op("y", lambda e, k=k, d=dst_ap, s=src_ap:
                                e.dma_start(out=d, in_=s),
                                deps=[dep], inc=s_sl, inck=16)
            g_state["dma"] += 1
            n -= 1

    def pump_gamma_act(n):
        while n > 0 and g_state["act"] < NG:
            pump_gamma_dma(4)
            if g_state["act"] >= g_state["dma"]:
                break
            k = g_state["act"]
            g_act_tgt[k] = b.op("v", lambda e, k=k:
                                e.tensor_reduce(gacc[:, k:k + 1],
                                                gbuf[k % 2][:], AX.X, ALU.add,
                                                apply_absolute_value=True),
                                deps=[g_dma_tgt[k]], inc=s_v)
            g_state["act"] += 1
            n -= 1
        pump_gamma_dma(4)

    stB2 = ExitStack()
    vpair = [sbuf(stB2, f"vpair{i}", [128, 16, 258], BF16) for i in range(2)]
    wqt = [sbuf(stB2, f"wqt{i}", [128, 2048], BF16) for i in range(2)]
    wkt = [sbuf(stB2, f"wkt{i}", [128, 2048], BF16) for i in range(2)]
    wvt = [sbuf(stB2, f"wvt{i}", [128, 2048], BF16) for i in range(2)]
    q_sb = sbuf(stB2, "q_sb", [128, 2048], BF16)
    k_sb = sbuf(stB2, "k_sb", [128, 2048], BF16)
    v_f32 = [sbuf(stB2, f"vf32{i}", [128, 512], F32) for i in range(2)]
    e1 = sbuf(stB2, "e1", [128, 512], BF16)
    e2 = sbuf(stB2, "e2", [128, 512], F32)

    vp_ones = [b.op("v", lambda e, i=i: e.memset(vpair[i][:, :, 128:129], 1.0),
                    inc=s_v) for i in range(2)]

    w_guard = {"q": [None, None], "k": [None, None], "v": [None, None]}
    w_load = {"q": [None] * H, "k": [None] * H, "v": [None] * H}
    ring_i = {"i": 0}
    q_sb_guard = None
    k_sb_guard = None
    vf_guard = [None, None]
    e_guard = [None, None]
    kv_done = [None] * 8
    vpair_mm = [None, None]

    def load_head_w(ic):
        j = ic % 2
        for nm, wt, buf in (("q", wq_t, wqt), ("k", wk_t, wkt),
                            ("v", wv_t, wvt)):
            w_load[nm][ic] = b.op("g", lambda e, ic=ic, j=j, wt=wt, buf=buf:
                                  e.dma_start(out=buf[j][:], in_=wt[ic]),
                                  deps=[w_guard[nm][j]], inc=s_gw, inck=16)

    load_head_w(0)
    load_head_w(1)

    vpair_evac_done = None
    for ic in range(H):
        j = ic % 2
        pair = ic // 2
        r0 = (ic % 2) * 64
        vb = pair % 2

        for nm in ("q", "k"):
            wtl = wqt if nm == "q" else wkt
            dest = q_sb if nm == "q" else k_sb
            bias_col = ic if nm == "q" else H + ic
            evac_deps = []
            dep = None
            for ct in range(NCT):
                slot = ring_i["i"] % 3
                ring_i["i"] += 1
                bt = banks.use(slot)
                for dc in range(NDC):
                    dep = b.op("t", lambda e, ct=ct, dc=dc, j=j, bt=bt, wtl=wtl:
                               e.matmul(bt[:], wtl[j][:, dc * 128:(dc + 1) * 128],
                                        buf1[:, dc, ct * 512:(ct + 1) * 512],
                                        start=(dc == 0), stop=(dc == NDC - 1)),
                               deps=[w_load[nm][ic], xn1_done] if dc == 0 else None,
                               inc=s_t if dc == NDC - 1 else None)
                gdep = b.op("s", lambda e, ct=ct, bt=bt, dest=dest, bc=bias_col:
                            e.activation(out=dest[:, ct * 512:(ct + 1) * 512],
                                         in_=bt[:], func=AF.Identity,
                                         bias=bias_s[:, bc:bc + 1]),
                            deps=[dep, consts_dep,
                                  q_sb_guard if nm == "q" else k_sb_guard],
                            inc=s_a)
                banks.done(slot, gdep)
                evac_deps.append(gdep)
            w_guard[nm][j] = dep

            if nm == "q":
                for ct in range(NCT):
                    mslot = 3 + ct % 2
                    bt = banks.use(mslot)
                    md = b.op("t", lambda e, ct=ct, bt=bt:
                              e.matmul(bt[0:64, :], wmq_s[:, :],
                                       q_sb[:, ct * 512:(ct + 1) * 512],
                                       start=True, stop=True),
                              deps=[evac_deps[ct], wmaps_dep], inc=s_t)
                    q_sb_guard = md
                    d1 = b.op("v", lambda e, bt=bt:
                              e.tensor_scalar_min(e1[0:64, :], bt[0:64, :], 0.0),
                              deps=[md, e_guard[0]], inc=s_v)
                    d2 = b.op("s", lambda e:
                              e.activation(out=e2[0:64, :], in_=e1[0:64, :],
                                           func=AF.Exp),
                              deps=[d1, e_guard[1]], inc=s_a)
                    e_guard[0] = d2
                    d3 = b.op("v", lambda e, ct=ct, bt=bt, r0=r0, pair=pair:
                              e.scalar_tensor_tensor(
                                  q2_t[r0:r0 + 64, pair,
                                       ct * 512:(ct + 1) * 512],
                                  bt[0:64, :], 0.0, e2[0:64, :],
                                  ALU.max, ALU.add),
                              deps=[d2], inc=s_v)
                    e_guard[1] = d3
                    banks.done(mslot, d3)
                    pump_gamma_act(1)
            else:
                for pk in range(2):
                    mslot = 3 + pk
                    bt = banks.use(mslot)
                    md = None
                    for si in range(8):
                        sub = pk * 8 + si
                        md = b.op("t", lambda e, sub=sub, si=si, bt=bt:
                                  e.matmul(bt[:, si * 64:(si + 1) * 64],
                                           k_sb[:, sub * 128:(sub + 1) * 128],
                                           wmk_s[:, :], start=True, stop=True),
                                  deps=[evac_deps[sub // 4], wmaps_dep]
                                  if sub % 4 == 0 else None,
                                  inc=s_t if si == 7 else None)
                    k_sb_guard = md
                    d1 = b.op("v", lambda e, bt=bt:
                              e.tensor_scalar_min(e1[:, :], bt[:, :], 0.0),
                              deps=[md, e_guard[0]], inc=s_v)
                    d2 = b.op("s", lambda e:
                              e.activation(out=e2[:, :], in_=e1[:, :],
                                           func=AF.Exp),
                              deps=[d1, e_guard[1]], inc=s_a)
                    e_guard[0] = d2
                    d3 = b.op("v", lambda e, pk=pk, ic=ic, bt=bt:
                              e.scalar_tensor_tensor(
                                  k2T[:, pk * 8:(pk + 1) * 8,
                                      ic * 64:(ic + 1) * 64],
                                  bt[:].rearrange("p (s q) -> p s q", s=8),
                                  0.0,
                                  e2[:].rearrange("p (s q) -> p s q", s=8),
                                  ALU.max, ALU.add),
                              deps=[d2], inc=s_v)
                    e_guard[1] = d3
                    banks.done(mslot, d3)
                    pump_gamma_act(2)

        dep = None
        for ct in range(NCT):
            slot = ring_i["i"] % 3
            ring_i["i"] += 1
            bt = banks.use(slot)
            for dc in range(NDC):
                dep = b.op("t", lambda e, ct=ct, dc=dc, j=j, bt=bt:
                           e.matmul(bt[:], wvt[j][:, dc * 128:(dc + 1) * 128],
                                    buf1[:, dc, ct * 512:(ct + 1) * 512],
                                    start=(dc == 0), stop=(dc == NDC - 1)),
                           deps=[w_load["v"][ic]] if dc == 0 else None,
                           inc=s_t if dc == NDC - 1 else None)
            jv = ct % 2
            vfd = b.op("s", lambda e, ct=ct, bt=bt, ic=ic, jv=jv:
                       e.activation(out=v_f32[jv][:], in_=bt[:], func=AF.Identity,
                                    bias=bias_s[:, 2 * H + ic:2 * H + ic + 1]),
                       deps=[dep, vf_guard[jv]], inc=s_a)
            banks.done(slot, vfd)
            tslot = 5 + ct % 2
            bt2 = banks.use(tslot)
            td = None
            for qtr in range(4):
                td = b.op("t", lambda e, qtr=qtr, bt2=bt2, jv=jv:
                          e.transpose(bt2[:, qtr * 128:(qtr + 1) * 128],
                                      v_f32[jv][:, qtr * 128:(qtr + 1) * 128],
                                      ident_s[:]),
                          deps=[vfd, consts_dep] if qtr == 0 else None,
                          inc=s_t if qtr == 3 else None)
            vf_guard[jv] = td
            cofs = 0 if ic % 2 == 0 else 129
            ed = b.op("v", lambda e, ct=ct, bt2=bt2, vb=vb, cofs=cofs:
                      e.tensor_copy(
                          vpair[vb][:, ct * 4:(ct + 1) * 4, cofs:cofs + 128],
                          bt2[:].rearrange("p (s q) -> p s q", s=4)),
                      deps=[td, vp_ones[vb], vpair_mm[vb]], inc=s_v)
            banks.done(tslot, ed)
            pump_gamma_act(1)
            if ct == NCT - 1:
                vpair_evac_done = ed
        w_guard["v"][j] = dep

        pump_gamma_act(3)

        if ic % 2 == 1:
            bt = banks.use(7)
            kd = None
            for sub in range(16):
                kd = b.op("t", lambda e, sub=sub, pair=pair, bt=bt, vb=vb:
                          e.matmul(bt[:, 0:258],
                                   k2T[:, sub, pair * 128:(pair + 1) * 128],
                                   vpair[vb][:, sub, :],
                                   start=(sub == 0), stop=(sub == 15)),
                          deps=[e_guard[1], vpair_evac_done]
                          if sub == 0 else None,
                          inc=s_t if sub == 15 else None)
            vpair_mm[vb] = kd
            kv_done[pair] = b.op("v", lambda e, pair=pair, bt=bt:
                                 e.tensor_copy(kv_sb[:, pair, :], bt[:, 0:258]),
                                 deps=[kd], inc=s_v)
            banks.done(7, kv_done[pair])
            pump_gamma_act(2)

        if ic + 2 < H:
            load_head_w(ic + 2)

    pump_gamma_act(NG)

    if debug == "q2":
        dump(lambda c: q2_t[:, c, :], 8, 2048, [e_guard[1], kv_done[7]])
    if debug == "k2T":
        dump(lambda c: k2T[:, c, :], 16, 1024, [e_guard[1], kv_done[7]])
    if debug == "kv":
        dump(lambda c: kv_sb[:, c, :], 8, 258, [kv_done[7]])

    stB2.close()
    stGB.close()

    stD = ExitStack()
    r128 = sbuf(stD, "r128", [128, 2048], F32)
    rows_fD = sbuf(stD, "rows_fD", [128, 2048], F32)
    rows_bD = sbuf(stD, "rows_bD", [128, 2048], BF16)
    lnrow = rows_fD[0:1, :]
    recrow = rows_bD[0:1, :]
    b.gate(("t", "v", "s"))

    r128_guard = None
    recrow_guard = None
    attn_done = [None] * H
    for h in range(H):
        pair = h // 2
        r0 = (h % 2) * 64
        c0 = 0 if h % 2 == 0 else 129
        row_deps = []
        for ct in range(NCT):
            slot = ct % 2
            bt = banks.use(slot)
            md = b.op("t", lambda e, ct=ct, bt=bt, pair=pair, r0=r0:
                      e.matmul(bt[0:1, :],
                               kv_sb[r0:r0 + 64, pair, 128:129],
                               q2_t[r0:r0 + 64, pair, ct * 512:(ct + 1) * 512],
                               start=True, stop=True),
                      deps=[kv_done[pair]], inc=s_t)
            ld = b.op("s", lambda e, ct=ct, bt=bt:
                      e.activation(out=lnrow[:, ct * 512:(ct + 1) * 512],
                                   in_=bt[0:1, :], func=AF.Ln,
                                   bias=cval_s[0:1, 1:2]),
                      deps=[md], inc=s_a)
            banks.done(slot, ld)
            row_deps.append(ld)
        rd = b.op("s", lambda e: e.activation(out=recrow[:], in_=lnrow[:],
                                              func=AF.Exp, scale=-1.0),
                  deps=row_deps + [recrow_guard], inc=s_a)
        ev = None
        for ct in range(NCT):
            slot = 2 + ct % 2
            bt = banks.use(slot)
            g = b.op("t", lambda e, ct=ct, bt=bt:
                     e.matmul(bt[:], ones_b[0:1, :],
                              recrow[:, ct * 512:(ct + 1) * 512],
                              start=True, stop=True),
                     deps=[rd], inc=s_t)
            recrow_guard = g
            ev = b.op("v", lambda e, ct=ct, bt=bt:
                      e.tensor_copy(r128[:, ct * 512:(ct + 1) * 512], bt[:]),
                      deps=[g, r128_guard], inc=s_v)
            banks.done(slot, ev)
        qsd = b.op("v", lambda e, pair=pair, r0=r0:
                   e.tensor_mul(q2s[r0:r0 + 64, pair, :],
                                q2_t[r0:r0 + 64, pair, :],
                                r128[r0:r0 + 64, :]),
                   deps=[ev], inc=s_v)
        r128_guard = qsd
        for ct in range(NCT):
            slot = 4 + ct % 2
            bt = banks.use(slot)
            md = b.op("t", lambda e, ct=ct, bt=bt, pair=pair, r0=r0, c0=c0:
                      e.matmul(bt[:], kv_sb[r0:r0 + 64, pair, c0:c0 + 128],
                               q2s[r0:r0 + 64, pair, ct * 512:(ct + 1) * 512],
                               start=True, stop=True),
                      deps=[qsd], inc=s_t)
            ad = b.op("s", lambda e, ct=ct, h=h, bt=bt:
                      e.activation(out=buf1[:, h, ct * 512:(ct + 1) * 512],
                                   in_=bt[:], func=AF.Copy),
                      deps=[md], inc=s_a)
            banks.done(slot, ad)
        attn_done[h] = ad

    if debug == "attn":
        dump(lambda c: buf1[:, c, :], NDC, 2048, [attn_done[H - 1]])

    stD.close()
    stQ.close()

    stF = ExitStack()
    sqF = [sbuf(stF, f"sqF{i}", [128, 2048], BF16) for i in range(2)]
    ybfF = [sbuf(stF, f"ybfF{i}", [128, 2048], BF16) for i in range(2)]
    T1F = sbuf(stF, "T1F", [128, 2048], F32)
    T2F = sbuf(stF, "T2F", [128, 2048], F32)
    SCRF = sbuf(stF, "SCRF", [128, 2048], F32)
    xtF = sbuf(stF, "xtF", [128, 2048], F32)
    ytF = [sbuf(stF, f"ytF{i}", [128, 2048], F32) for i in range(2)]
    fs1 = sbuf(stF, "fs1", [128, 2048], F32)
    fs2 = sbuf(stF, "fs2", [128, 2048], F32)
    x2bf = sbuf(stF, "x2bf", [128, 2048], BF16)
    stE2 = ExitStack()
    wot = [sbuf(stE2, f"wot{i}", [128, 2048], BF16) for i in range(2)]
    ytile = [sbuf(stE2, f"ytile{i}", [128, 2048], F32) for i in range(2)]
    ytile_guard = [None, None]
    ybf_guard = [None, None]
    sqg = [None, None]
    b.gate(("g", "s", "v"))
    wot_guard = [None, None]
    wot_load = [None] * H
    y_done = [None] * NDC

    def load_wo(ic):
        jj = ic % 2
        wot_load[ic] = b.op("g", lambda e, ic=ic, jj=jj:
                            e.dma_start(out=wot[jj][:], in_=wo_t[ic]),
                            deps=[wot_guard[jj]], inc=s_gw, inck=16)

    load_wo(0)
    load_wo(1)
    for ic in range(H):
        jj = ic % 2
        dep = None
        for ct in range(NCT):
            slot = ring_i["i"] % 3
            ring_i["i"] += 1
            bt = banks.use(slot)
            for dc in range(NDC):
                dep = b.op("t", lambda e, ct=ct, dc=dc, jj=jj, bt=bt:
                           e.matmul(bt[:], wot[jj][:, dc * 128:(dc + 1) * 128],
                                    buf1[:, dc, ct * 512:(ct + 1) * 512],
                                    start=(dc == 0), stop=(dc == NDC - 1)),
                           deps=[wot_load[ic], attn_done[dc]]
                           if dc == 0 else None,
                           inc=s_t if dc == NDC - 1 else None)
            yd = b.op("v", lambda e, ct=ct, ic=ic, jj=jj, bt=bt:
                      e.tensor_scalar(ytile[jj][:, ct * 512:(ct + 1) * 512],
                                      bt[:],
                                      bias_s[:, 3 * H + ic:3 * H + ic + 1],
                                      None, ALU.add),
                      deps=[dep, ytile_guard[jj]], inc=s_v)
            banks.done(slot, yd)
        y_done[ic] = b.op("g", lambda e, ic=ic, jj=jj:
                          e.dma_start(out=y_scr[ic], in_=ytile[jj][:]),
                          deps=[yd], inc=s_st, inck=16)
        ycast = b.op("v", lambda e, jj=jj:
                     e.tensor_copy(ybfF[jj][:], ytile[jj][:]),
                     deps=[yd, ybf_guard[jj]], inc=s_v)
        ytile_guard[jj] = [y_done[ic], ycast]
        ysq = b.op("s", lambda e, jj=jj:
                   e.activation(out=sqF[jj][:], in_=ybfF[jj][:],
                                func=AF.Square),
                   deps=[ycast, sqg[jj]], inc=s_a)
        st_dep = None
        for ct in range(NCT):
            bt = banks.use(3 + ct) if ic == 0 else pb[3 + ct]
            st_dep = b.op("t", lambda e, ic=ic, ct=ct, jj=jj, bt=bt:
                          e.matmul(bt[0:1, :], ones_b[:, 0:1],
                                   ybfF[jj][:, ct * 512:(ct + 1) * 512],
                                   start=(ic == 0), stop=(ic == H - 1)),
                          deps=[ysq] if ct == 0 else None,
                          inc=s_t if ct == NCT - 1 else None)
        for ct in range(1):
            bt = banks.use(7) if ic == 0 else pb[7]
            st_dep = b.op("t", lambda e, ic=ic, jj=jj, bt=bt:
                          e.matmul(bt[0:1, :], ones_b[:, 0:1],
                                   sqF[jj][:, 0:512],
                                   start=(ic == 0), stop=(ic == H - 1)),
                          inc=s_t)
        ybf_guard[jj] = st_dep
        sqg[jj] = st_dep
        wot_guard[jj] = dep
        if ic + 2 < H:
            load_wo(ic + 2)

    stE2.close()

    b.gate(("g",))
    yt_guard = [None, None]
    f_stats = None
    p2sq = [None, None]
    for ic in range(NDC):
        j = ic % 2
        ld = b.op("y", lambda e, ic=ic, j=j:
                  e.dma_start(out=ytF[j][:], in_=y_scr[ic]),
                  deps=[y_done[ic], yt_guard[j]], inc=s_sl, inck=16)
        ycast = b.op("v", lambda e, j=j:
                     e.tensor_copy(ybfF[j][:, 512:2048],
                                   ytF[j][:, 512:2048]),
                     deps=[ld, p2sq[j]], inc=s_v)
        yt_guard[j] = ycast
        ysq = b.op("s", lambda e, j=j:
                   e.activation(out=sqF[j][:, 512:2048],
                                in_=ybfF[j][:, 512:2048], func=AF.Square),
                   deps=[ycast], inc=s_a)
        for ct in range(3):
            bt = banks.use(ct) if ic == 0 else pb[ct]
            f_stats = b.op("t", lambda e, ic=ic, ct=ct, j=j, bt=bt:
                           e.matmul(bt[0:1, :], ones_b[:, 0:1],
                                    sqF[j][:, 512 + ct * 512:
                                            1024 + ct * 512],
                                    start=(ic == 0), stop=(ic == NDC - 1)),
                           deps=[ysq] if ct == 0 else None,
                           inc=s_t if ct == 2 else None)
        p2sq[j] = f_stats
        rstdF, mrF, f_bcast = ln_finish(f_stats, T1F, T2F, SCRF,
                                    xb=[3, 4, 5, 6], yb=[7, 0, 1, 2])

    x2_store = [None] * NDC
    prev_x2d = None
    prev_store = None
    prev_cbd = None
    prev_mm = None
    prev_sqd = None
    x3_mm = None
    for dc in range(NDC):
        j = dc % 2
        ld = b.op("y", lambda e, dc=dc:
                  e.dma_start(out=xtF[:], in_=xT[dc * 128:(dc + 1) * 128, :]),
                  deps=[prev_x2d], inc=s_sl, inck=16)
        yld = b.op("y", lambda e, dc=dc, j=j:
                   e.dma_start(out=ytF[j][:], in_=y_scr[dc]),
                   deps=[yt_guard[j]], inc=s_sl, inck=16)
        b.op("v", lambda e, dc=dc, j=j: e.tensor_mul(fs1[:], ytF[j][:], rstdF),
             deps=[f_bcast, prev_store, yld])
        sub_d = b.op("v", lambda e: e.tensor_sub(fs2[:], fs1[:], mrF),
                     inc=s_v)
        x2d = b.op("g", lambda e: e.tensor_add(fs1[:], fs2[:], xtF[:]),
                   deps=[ld, sub_d], inc=s_p)
        yt_guard[j] = x2d
        prev_x2d = x2d
        x2_store[dc] = b.op("g", lambda e, dc=dc:
                            e.dma_start(out=x2_scr[dc], in_=fs1[:]),
                            inc=s_st, inck=16)
        prev_store = x2_store[dc]
        cbd = b.op("v", lambda e: e.tensor_copy(x2bf[:], fs1[:]),
                   deps=[prev_mm, prev_sqd, x2d], inc=s_v)
        prev_cbd = cbd
        sqd = b.op("s", lambda e, j=j:
                   e.activation(out=sqF[j][:], in_=x2bf[:], func=AF.Square),
                   deps=[cbd], inc=s_a)
        prev_sqd = sqd
        for ct in range(NCT):
            bt = banks.use(ct) if dc == 0 else pb[ct]
            b.op("t", lambda e, dc=dc, ct=ct, bt=bt:
                 e.matmul(bt[0:1, :], ones_b[:, 0:1],
                          x2bf[:, ct * 512:(ct + 1) * 512],
                          start=(dc == 0), stop=(dc == NDC - 1)),
                 deps=[sqd] if ct == 0 else None)
        for ct in range(NCT):
            bt = banks.use(4 + ct) if dc == 0 else pb[4 + ct]
            x3_mm = b.op("t", lambda e, dc=dc, ct=ct, j=j, bt=bt:
                         e.matmul(bt[0:1, :], ones_b[:, 0:1],
                                  sqF[j][:, ct * 512:(ct + 1) * 512],
                                  start=(dc == 0), stop=(dc == NDC - 1)),
                         inc=s_t if ct == NCT - 1 else None)
        prev_mm = x3_mm

    rstd3, mr3, x3_bcast = ln_finish(x3_mm, T1F, T2F, SCRF)

    xn2_done = None
    prev_n = None
    for dc in range(NDC):
        ld = b.op("y", lambda e, dc=dc:
                  e.dma_start(out=xtF[:], in_=x2_scr[dc]),
                  deps=[prev_x2d if dc == 0 else prev_n, x2_store[dc]],
                  inc=s_sl, inck=16)
        b.op("v", lambda e: e.tensor_mul(fs2[:], xtF[:], rstd3),
             deps=[ld, x3_bcast])
        xn2_done = b.op("v", lambda e, dc=dc:
                        e.tensor_sub(buf1[:, dc, :], fs2[:], mr3),
                        inc=s_v)
        prev_n = xn2_done

    if debug == "xn2":
        dump(lambda c: buf1[:, c, :], NDC, 2048, [xn2_done])

    stF.close()

    stT = ExitStack()
    gcol = sbuf(stT, "gcol", [128, 3], F32)
    throwt = sbuf(stT, "throwt", [128, 8], F32)
    throw = throwt[0:1, :]
    gamma_last = (s_a, g_act_tgt[NG - 1][1])
    t2 = None
    for w in range(3):
        rdep = b.op("v", lambda e, w=w:
                    e.tensor_reduce(gcol[:, w:w + 1],
                                    gacc[:, w * 88:(w + 1) * 88],
                                    AX.X, ALU.add),
                    deps=[gamma_last], inc=s_v)
        bt = banks.use(w % 2)
        md = b.op("t", lambda e, w=w, bt=bt:
                  e.matmul(bt[0:1, 0:1], ones_s[:, 0:1], gcol[:, w:w + 1],
                           start=True, stop=True),
                  deps=[rdep], inc=s_t)
        t1 = b.op("s", lambda e, w=w, bt=bt:
                  e.activation(out=throw[:, 2 * w:2 * w + 1], in_=bt[0:1, 0:1],
                               func=AF.Identity, scale=0.5 / NW,
                               bias=cval_s[0:1, 2:3]),
                  deps=[md], inc=s_a)
        banks.done(w % 2, t1)
        t2 = b.op("s", lambda e, w=w:
                  e.activation(out=throw[:, 2 * w + 1:2 * w + 2],
                               in_=throw[:, 2 * w:2 * w + 1],
                               func=AF.Copy, scale=-1.0),
                  inc=s_a)
    bt = banks.use(2)
    md = b.op("t", lambda e, bt=bt:
              e.matmul(bt[:, 0:6], ones_s[0:1, :], throw[:, 0:6],
                       start=True, stop=True),
              deps=[t2], inc=s_t)
    thr_dep = b.op("v", lambda e, bt=bt: e.tensor_copy(thrs[:], bt[:, 0:6]),
                   deps=[md], inc=s_v)
    banks.done(2, thr_dep)
    stT.close()

    stN = ExitStack()
    h_b = sbuf(stN, "h_b", [128, NHC, 512], BF16)
    f_b = sbuf(stN, "f_b", [128, NDC, 512], F32)
    w1f = sbuf(stN, "w1f", [128, 2048], F32)
    w2f = sbuf(stN, "w2f", [128, 2048], F32)
    qw1 = [sbuf(stN, f"qw1_{i}", [128, 2048], BF16) for i in range(2)]
    qw2 = [sbuf(stN, f"qw2_{i}", [128, 2048], BF16) for i in range(2)]
    w3f = [sbuf(stN, f"w3f{i}", [128, 1024], F32) for i in range(2)]
    qw3g = [sbuf(stN, f"qw3g{i}", [128, 2048], BF16) for i in range(2)]
    qw3g_dep = [None, None]
    qw3g_guard = [None, None]
    sgt = [sbuf(stN, "sgt0", [128, 512], F32)] * 2
    x2t = [sbuf(stN, "x2t0", [128, 512], F32)] * 2
    of1 = sbuf(stN, "of1", [128, 512], F32)
    of2 = sbuf(stN, "of2", [128, 512], F32)
    sqN = sbuf(stN, "sqN", [128, 2048], BF16)
    qtb = sqN
    fbf = [sqN[:, 1024 + i * 512:1536 + i * 512] for i in range(2)]
    T1N = sbuf(stN, "T1N", [128, 512], F32)
    T2N = sbuf(stN, "T2N", [128, 512], F32)
    SCRN = of2
    b.gate()

    w1f_guard = None
    w2f_guard = None
    qw_guard = [[None, None], [None, None]]
    w3f_guard = [None, None]
    qw3_guard = [None, None]
    sgt_guard = [None, None]
    x2t_guard = [None, None]
    h_guard = [None] * NHC
    f_guard = [None] * NDC
    qtb_state = {"g": None}
    out_stores = []
    qw1_store = [None] * NHC
    qw2_store = [None] * NHC
    qw3_store = [[None] * NHC for _ in range(4)]

    def quantize(load_dep, wf_ap, qw_ap, widx, qg):
        d1 = b.op("v", lambda e, wf=wf_ap: e.tensor_scalar(
            qtb[:, 0:wf.shape[1]], wf,
            thrs[:, 2 * widx + 1:2 * widx + 2], None, ALU.is_le),
            deps=[load_dep, thr_dep, qtb_state["g"]], inc=s_v)
        d2 = b.op("v", lambda e, wf=wf_ap, qw=qw_ap: e.scalar_tensor_tensor(
            qw, wf, thrs[:, 2 * widx:2 * widx + 1], qtb[:, 0:wf.shape[1]],
            ALU.is_ge, ALU.subtract), deps=[qg], inc=s_v)
        qtb_state["g"] = d2
        return d2

    for blk in range(NB):
        c0 = blk * 512
        for hc in range(NHC):
            j = hc % 2
            if blk == 0:
                l1 = b.op("y", lambda e, hc=hc:
                          e.dma_start(out=w1f[:], in_=w1_t[hc]),
                          deps=[w1f_guard], inc=s_sl, inck=16)
                q1d = quantize(l1, w1f[:], qw1[j][:], 0, qw_guard[0][j])
                w1f_guard = q1d
                qw1_store[hc] = b.op("g", lambda e, hc=hc, j=j:
                     e.dma_start(out=qw1_scr[hc], in_=qw1[j][:]),
                     deps=[q1d], inc=s_st, inck=16)
                qw_guard[0][j] = [qw_guard[0][j], qw1_store[hc]]
                l2 = b.op("y", lambda e, hc=hc:
                          e.dma_start(out=w2f[:], in_=w2_t[hc]),
                          deps=[w2f_guard], inc=s_sl, inck=16)
                q2d = quantize(l2, w2f[:], qw2[j][:], 1, qw_guard[1][j])
                w2f_guard = q2d
                qw2_store[hc] = b.op("g", lambda e, hc=hc, j=j:
                     e.dma_start(out=qw2_scr[hc], in_=qw2[j][:]),
                     deps=[q2d], inc=s_st, inck=16)
                qw_guard[1][j] = [qw_guard[1][j], qw2_store[hc]]
                qw1_dep, qw2_dep = q1d, q2d
            else:
                qw1_dep = b.op("y", lambda e, hc=hc, j=j:
                               e.dma_start(out=qw1[j][:], in_=qw1_scr[hc]),
                               deps=[qw_guard[0][j], qw1_store[hc]],
                               inc=s_sl, inck=16)
                qw2_dep = b.op("y", lambda e, hc=hc, j=j:
                               e.dma_start(out=qw2[j][:], in_=qw2_scr[hc]),
                               deps=[qw_guard[1][j], qw2_store[hc]],
                               inc=s_sl, inck=16)
            sA = (hc % 3) * 2
            sB = sA + 1
            btA = banks.use(sA)
            btB = banks.use(sB)
            dA = dB = None
            for dc in range(NDC):
                dA = b.op("t", lambda e, dc=dc, j=j, btA=btA, c0=c0:
                          e.matmul(btA[:], qw1[j][:, dc * 128:(dc + 1) * 128],
                                   buf1[:, dc, c0:c0 + 512],
                                   start=(dc == 0), stop=(dc == NDC - 1)),
                          deps=[qw1_dep, xn2_done] if dc == 0 else None,
                          inc=s_t if dc == NDC - 1 else None)
            for dc in range(NDC):
                dB = b.op("t", lambda e, dc=dc, j=j, btB=btB, c0=c0:
                          e.matmul(btB[:], qw2[j][:, dc * 128:(dc + 1) * 128],
                                   buf1[:, dc, c0:c0 + 512],
                                   start=(dc == 0), stop=(dc == NDC - 1)),
                          deps=[qw2_dep] if dc == 0 else None,
                          inc=s_t if dc == NDC - 1 else None)
            if blk == 0:
                qw_guard[0][j] = [dA, qw1_store[hc]]
                qw_guard[1][j] = [dB, qw2_store[hc]]
            else:
                qw_guard[0][j] = dA
                qw_guard[1][j] = dB
            sg = b.op("s", lambda e, j=j, btA=btA:
                      e.activation(out=sgt[j][:], in_=btA[:], func=AF.Sigmoid),
                      deps=[dA, sgt_guard[0]], inc=s_a)
            banks.done(sA, sg)
            hd = b.op("v", lambda e, hc=hc, j=j, btB=btB:
                      e.tensor_mul(h_b[:, hc, :], sgt[j][:], btB[:]),
                      deps=[sg, dB, h_guard[hc]], inc=s_v)
            sgt_guard[0] = hd
            banks.done(sB, hd)
            h_guard[hc] = hd

        if debug == "h_b" and blk == 1:
            dump(lambda c: h_b[:, c, :], NHC, 512, [h_guard[NHC - 1]])

        for g4 in range(4):
            gb = (g4 % 2) * 4
            f_mm = None
            for hc in range(NHC):
                jg = (hc // 4) % 2
                if blk == 0:
                    if hc % 2 == 0:
                        jw = (hc // 2) % 2
                        l3 = b.op("y", lambda e, g4=g4, hc=hc, jw=jw:
                                  e.dma_start(
                                      out=w3f[jw][:].rearrange(
                                          "p (h c) -> p h c", h=2),
                                      in_=w3_t[g4, hc:hc + 2].rearrange(
                                          "h p c -> p h c")),
                                  deps=[w3f_guard[jw]], inc=s_sl, inck=16)
                        dst = qw3g[jg][:, (hc % 4) * 512:(hc % 4) * 512 + 1024]
                        q3d = quantize(l3, w3f[jw][:], dst, 2,
                                       qw3g_guard[jg] if hc % 4 == 0 else None)
                        w3f_guard[jw] = q3d
                        qw3_dep = q3d
                    if hc % 4 == 2:
                        qst = b.op("g", lambda e, g4=g4, hc=hc, jg=jg:
                                   e.dma_start(out=qw3_scr[g4, hc // 4],
                                               in_=qw3g[jg][:]),
                                   deps=[q3d], inc=s_st, inck=16)
                        for i in range(4):
                            qw3_store[g4][(hc // 4) * 4 + i] = qst
                else:
                    if hc % 4 == 0:
                        qw3g_dep[jg] = b.op(
                            "y", lambda e, g4=g4, hc=hc, jg=jg:
                            e.dma_start(out=qw3g[jg][:],
                                        in_=qw3_scr[g4, hc // 4]),
                            deps=[qw3g_guard[jg], qw3_store[g4][hc]],
                            inc=s_sl, inck=16)
                    qw3_dep = qw3g_dep[jg]
                cof = (hc % 4) * 512
                for i4 in range(4):
                    bt = banks.use(gb + i4) if hc == 0 else pb[gb + i4]
                    f_mm = b.op("t", lambda e, hc=hc, i4=i4, jg=jg, cof=cof,
                                bt=bt:
                                e.matmul(bt[:],
                                         qw3g[jg][:, cof + i4 * 128:
                                                   cof + (i4 + 1) * 128],
                                         h_b[:, hc, :],
                                         start=(hc == 0), stop=(hc == NHC - 1)),
                                deps=[qw3_dep, h_guard[hc]]
                                if i4 == 0 else None,
                                inc=s_t if i4 == 3 else None)
                if hc % 4 == 3 or hc == NHC - 1:
                    if blk == 0:
                        qw3g_guard[jg] = [f_mm, qw3_store[g4][hc]]
                    else:
                        qw3g_guard[jg] = f_mm
                h_guard[hc] = f_mm
            for i4 in range(4):
                ic = g4 * 4 + i4
                fd = b.op("s", lambda e, ic=ic, i4=i4, gb=gb:
                          e.activation(out=f_b[:, ic, :], in_=pb[gb + i4][:],
                                       func=AF.Copy),
                          deps=[f_mm, f_guard[ic]], inc=s_a)
                banks.done(gb + i4, fd)
                f_guard[ic] = fd

        fsq_guard = [None, None]
        st_mm = None
        for ic in range(NDC):
            j = ic % 2
            fcd = b.op("v", lambda e, ic=ic, j=j:
                       e.tensor_copy(fbf[j], f_b[:, ic, :]),
                       deps=[f_guard[ic], fsq_guard[j]], inc=s_v)
            sqd = b.op("s", lambda e, j=j:
                       e.activation(out=sqN[:, j * 512:(j + 1) * 512],
                                    in_=fbf[j], func=AF.Square),
                       deps=[fcd], inc=s_a)
            bt0 = banks.use(6) if ic == 0 else pb[6]
            b.op("t", lambda e, j=j, ic=ic, bt0=bt0:
                 e.matmul(bt0[0:1, :], ones_b[:, 0:1], fbf[j],
                          start=(ic == 0), stop=(ic == NDC - 1)),
                 deps=[sqd] if True else None)
            bt1 = banks.use(7) if ic == 0 else pb[7]
            st_mm = b.op("t", lambda e, ic=ic, j=j, bt1=bt1:
                         e.matmul(bt1[0:1, :], ones_b[:, 0:1],
                                  sqN[:, j * 512:(j + 1) * 512],
                                  start=(ic == 0), stop=(ic == NDC - 1)),
                         inc=s_t)
            fsq_guard[j] = st_mm
        rstdN, mrN, b4 = ln_finish(st_mm, T1N, T2N, SCRN, width=512,
                                   xb=[6], yb=[7])

        for ic in range(NDC):
            ld = b.op("g", lambda e, ic=ic, c0=c0:
                      e.dma_start(out=x2t[0][:],
                                  in_=x2_scr[ic, :, c0:c0 + 512]),
                      deps=[x2_store[ic]], inc=s_st, inck=16)
            b.op("g", lambda e, ic=ic: e.tensor_mul(of1[:], f_b[:, ic, :],
                                                    rstdN),
                 deps=[b4, out_stores[-1] if out_stores else None])
            b.op("g", lambda e: e.tensor_sub(of2[:], of1[:], mrN))
            od = b.op("g", lambda e: e.tensor_add(of1[:], of2[:], x2t[0][:]),
                      deps=[ld], inc=s_p)
            f_guard[ic] = od
            sd = b.op("g", lambda e, ic=ic, c0=c0:
                      e.dma_start(out=out_t[ic, :, c0:c0 + 512], in_=of1[:]),
                      inc=s_st, inck=16)
            out_stores.append(sd)

    if debug == "qw1":
        for hc in range(16):
            b.op("g", lambda e, hc=hc:
                 e.dma_start(out=dbg[hc], in_=qw1_scr[hc]),
                 deps=[(s_st, s_st.n)], inc=s_st, inck=16)

    b.wait("g", (s_st, s_st.n))
    stN.close()

    with nc.Block() as block:
        @block.tensor
        def _(t):
            for f in b.ops["t"]:
                f(t)

        @block.vector
        def _(v):
            for f in b.ops["v"]:
                f(v)

        @block.scalar
        def _(s):
            for f in b.ops["s"]:
                f(s)

        @block.gpsimd
        def _(g):
            for f in b.ops["g"]:
                f(g)

        @block.sync
        def _(y):
            for f in b.ops["y"]:
                f(y)

    nc._dbg_sems = {"s_t": s_t.n, "s_v": s_v.n, "s_a": s_a.n,
                    "s_gw": s_gw.n, "s_sl": s_sl.n, "s_st": s_st.n}
    top.close()
    return nc


def _tile_proj(w):
    O, I = w.shape
    noc = O // 128
    ndc = I // 128
    return np.ascontiguousarray(
        w.reshape(noc, 128, ndc, 128).transpose(0, 3, 2, 1).reshape(noc, 128, I))


def _tile_w3(w3):
    w3T = np.ascontiguousarray(w3.T)
    return np.ascontiguousarray(
        w3T.reshape(44, 128, 4, 512).transpose(2, 0, 1, 3))


_nc_cache = {}


def _get_nc(debug=None):
    if debug not in _nc_cache:
        _nc_cache[debug] = build_nc(debug=debug)
    return _nc_cache[debug]


def make_inputs(ins):
    f32 = np.float32
    shared = {
        "wq_t": _tile_proj(np.asarray(ins["L0_attn_wq"], f32)),
        "wk_t": _tile_proj(np.asarray(ins["L0_attn_wk"], f32)),
        "wv_t": _tile_proj(np.asarray(ins["L0_attn_wv"], f32)),
        "wo_t": _tile_proj(np.asarray(ins["L0_attn_wo"], f32)),
        "wmq_t": np.ascontiguousarray(np.asarray(ins["L0_attn_wmap_q"], f32).T),
        "wmk_t": np.ascontiguousarray(np.asarray(ins["L0_attn_wmap_k"], f32).T),
        "bias_t": np.ascontiguousarray(np.concatenate([
            np.asarray(ins[k], f32).reshape(16, 128).T
            for k in ("L0_attn_bq", "L0_attn_bk", "L0_attn_bv", "L0_attn_bo")],
            axis=1)),
        "w1_t": _tile_proj(np.asarray(ins["L0_ffn_w1"], f32)),
        "w2_t": _tile_proj(np.asarray(ins["L0_ffn_w2"], f32)),
        "w3_t": _tile_w3(np.asarray(ins["L0_ffn_w3"], f32)),
        "ones_bf": np.ones((128, 128), ml_dtypes.bfloat16),
        "ones_f": np.ones((128, 128), np.float32),
        "ident_f": np.eye(128, dtype=np.float32),
        "cvals": np.broadcast_to(
            np.array([LN_EPS, ATTN_EPS, FFN_HALF_EPS, 1.0 / 2048, 0, 0, 0, 0],
                     np.float32), (128, 8)).copy(),
    }
    x = np.asarray(ins["L0_attn_hidden_states"], np.float32)
    in_maps = []
    for c in range(8):
        m = dict(shared)
        m["xT"] = np.ascontiguousarray(x[c].T)
        in_maps.append(m)
    return in_maps


def kernel(**ins):
    nc = _get_nc(DEBUG)
    in_maps = make_inputs(ins)
    res = run_bass_kernel_spmd(nc, in_maps, core_ids=list(range(8)))
    out = np.empty((8, 2048, 2048), np.float32)
    for c in range(8):
        o = np.asarray(res.results[c]["out_t"], np.float32).reshape(2048, 2048)
        out[c] = o.T
    if DEBUG is not None:
        kernel.last_results = res
    return out


# version 06
# speedup vs baseline: 1.0576x; 1.0576x over previous
import numpy as np
import ml_dtypes
from contextlib import ExitStack

import concourse.bass as bass
import concourse.mybir as mybir
from concourse.bass_utils import run_bass_kernel_spmd

F32 = mybir.dt.float32
BF16 = mybir.dt.bfloat16
AF = mybir.ActivationFunctionType
ALU = mybir.AluOpType
AX = mybir.AxisListType

D = 2048
S = 2048
H = 16
DM = 64
HID = 5632
NDC = D // 128
NCT = S // 512
NHC = HID // 128
TB = 512
NB = S // TB
LN_EPS = 1e-5
ATTN_EPS = 1e-6
FFN_HALF_EPS = 0.5e-8
NW = float(HID * D)

DEBUG = None


class Sem:
    def __init__(self, h):
        self.h = h
        self.n = 0

    def bump(self, k=1):
        self.n += k
        return self.n


class B:
    ENGS = ("t", "v", "s", "g", "y")

    def __init__(self, nc):
        self.nc = nc
        self.ops = {k: [] for k in self.ENGS}
        self.waited = {k: {} for k in self.ENGS}
        self.all_sems = []

    def wait(self, eng, deps):
        if deps is None:
            return
        if isinstance(deps, tuple):
            deps = [deps]
        flat = []
        for d in deps:
            if isinstance(d, list):
                flat.extend(d)
            else:
                flat.append(d)
        for d in flat:
            if d is None:
                continue
            sem, t = d
            if self.waited[eng].get(id(sem), 0) >= t:
                continue
            self.waited[eng][id(sem)] = t
            self.ops[eng].append(lambda e, s=sem, t=t: e.wait_ge(s.h, t))

    def op(self, eng, fn, deps=None, inc=None, inck=1):
        self.wait(eng, deps)
        if inc is not None:
            tgt = inc.bump(inck)
            self.ops[eng].append(
                lambda e, fn=fn, s=inc, k=inck: fn(e).then_inc(s.h, k))
            return (inc, tgt)
        self.ops[eng].append(fn)
        return None

    def gate(self, engs=None):
        deps = [(s, s.n) for s in self.all_sems if s.n > 0]
        for eng in (engs or self.ENGS):
            self.wait(eng, deps)


class BankMgr:
    def __init__(self, b, tensors):
        self.b = b
        self.t = tensors
        self.last = [None] * 8

    def use(self, idx, eng="t"):
        self.b.wait(eng, self.last[idx])
        return self.t[idx]

    def done(self, idx, dep):
        self.last[idx] = dep


def build_nc(debug=None):
    nc = bass.Bass("TRN2", target_bir_lowering=False)
    dt = nc.dram_tensor

    xT = dt("xT", [D, S], F32, kind="ExternalInput").ap()
    wq_t = dt("wq_t", [H, 128, D], F32, kind="ExternalInput").ap()
    wk_t = dt("wk_t", [H, 128, D], F32, kind="ExternalInput").ap()
    wv_t = dt("wv_t", [H, 128, D], F32, kind="ExternalInput").ap()
    wo_t = dt("wo_t", [H, 128, D], F32, kind="ExternalInput").ap()
    wmq_t = dt("wmq_t", [128, DM], F32, kind="ExternalInput").ap()
    wmk_t = dt("wmk_t", [128, DM], F32, kind="ExternalInput").ap()
    bias_t = dt("bias_t", [128, 4 * H], F32, kind="ExternalInput").ap()
    w1_t = dt("w1_t", [NHC, 128, D], F32, kind="ExternalInput").ap()
    w2_t = dt("w2_t", [NHC, 128, D], F32, kind="ExternalInput").ap()
    w3_t = dt("w3_t", [4, NHC, 128, 512], F32, kind="ExternalInput").ap()
    ones_bf = dt("ones_bf", [128, 128], BF16, kind="ExternalInput").ap()
    ones_f = dt("ones_f", [128, 128], F32, kind="ExternalInput").ap()
    ident_f = dt("ident_f", [128, 128], F32, kind="ExternalInput").ap()
    cvals = dt("cvals", [128, 8], F32, kind="ExternalInput").ap()

    out_t = dt("out_t", [NDC, 128, S], F32, kind="ExternalOutput").ap()
    x2_scr = dt("x2_scr", [NDC, 128, S], F32,
                kind="ExternalOutput" if debug == "x2dump" else "Internal").ap()
    y_scr = dt("y_scr", [NDC, 128, S], F32,
               kind="ExternalOutput" if debug == "ydump" else "Internal").ap()
    qw1_scr = dt("qw1_scr", [NHC, 128, D], BF16, kind="Internal").ap()
    qw2_scr = dt("qw2_scr", [NHC, 128, D], BF16, kind="Internal").ap()
    qw3_scr = dt("qw3_scr", [4, 11, 128, 2048], BF16, kind="Internal").ap()

    dbg = None
    if debug is not None:
        dbg = dt("dbg", [64, 128, 2048], F32, kind="ExternalOutput").ap()

    b = B(nc)
    top = ExitStack()

    def mksem(name):
        s = Sem(top.enter_context(nc.semaphore(name)))
        b.all_sems.append(s)
        return s

    s_t = mksem("s_t")
    s_v = mksem("s_v")
    s_a = mksem("s_a")
    s_gw = mksem("s_gw")
    s_sl = mksem("s_sl")
    s_st = mksem("s_st")
    s_p = mksem("s_p")

    def sbuf(stk, name, shape, dtype):
        return stk.enter_context(nc.sbuf_tensor(name, shape, dtype))

    ones_b = sbuf(top, "ones_b", [128, 128], BF16)
    ones_s = sbuf(top, "ones_s", [128, 128], F32)
    ident_s = sbuf(top, "ident_s", [128, 128], F32)
    cval_s = sbuf(top, "cval_s", [128, 8], F32)
    bias_s = sbuf(top, "bias_s", [128, 4 * H], F32)
    wmq_s = sbuf(top, "wmq_s", [128, DM], BF16)
    wmk_s = sbuf(top, "wmk_s", [128, DM], BF16)
    buf1 = sbuf(top, "buf1", [128, NDC, 2048], BF16)
    gacc = sbuf(top, "gacc", [128, 264], F32)
    thrs = sbuf(top, "thrs", [128, 6], F32)

    pb = [top.enter_context(nc.psum_tensor(f"pb{i}", [128, 512], F32))
          for i in range(8)]
    banks = BankMgr(b, pb)

    dbg_n = {"i": 0}

    def dump(src_fn, nchunk, width, deps):
        if dbg is None:
            return
        for c in range(nchunk):
            i = dbg_n["i"]
            dbg_n["i"] += 1
            b.op("g", lambda e, c=c, i=i, w=width:
                 e.dma_start(out=dbg[i, :, 0:w], in_=src_fn(c)),
                 deps=deps, inc=s_st, inck=16)

    for dst, src in ((ones_b, ones_bf), (ones_s, ones_f), (ident_s, ident_f),
                     (cval_s, cvals), (bias_s, bias_t)):
        b.op("y", lambda e, d=dst, s=src: e.dma_start(out=d[:], in_=s),
             inc=s_sl, inck=16)
    consts_dep = (s_sl, s_sl.n)
    for dst, src in ((wmq_s, wmq_t), (wmk_s, wmk_t)):
        b.op("g", lambda e, d=dst, s=src: e.dma_start(out=d[:], in_=s),
             inc=s_gw, inck=16)
    wmaps_dep = (s_gw, s_gw.n)

    rows_guard = []

    def stats_pass(src, sq_scr, chunk_dep, nchunk=NDC, width=2048):
        nct = width // 512
        sq_guard = [None, None]
        mm_dep = None
        for dc in range(nchunk):
            j = dc % 2
            sqd = b.op("s", lambda e, dc=dc, j=j, w=width:
                       e.activation(out=sq_scr[j][:, 0:w], in_=src(dc),
                                    func=AF.Square),
                       deps=[chunk_dep(dc), sq_guard[j]], inc=s_a)
            for ct in range(nct):
                bt = banks.use(ct) if dc == 0 else pb[ct]
                b.op("t", lambda e, dc=dc, ct=ct, bt=bt:
                     e.matmul(bt[0:1, :], ones_b[:, 0:1],
                              src(dc)[:, ct * 512:(ct + 1) * 512],
                              start=(dc == 0), stop=(dc == nchunk - 1)),
                     deps=[sqd] if ct == 0 else None)
            for ct in range(nct):
                bt = banks.use(4 + ct) if dc == 0 else pb[4 + ct]
                mm_dep = b.op("t", lambda e, dc=dc, ct=ct, j=j, bt=bt:
                              e.matmul(bt[0:1, :], ones_b[:, 0:1],
                                       sq_scr[j][:, ct * 512:(ct + 1) * 512],
                                       start=(dc == 0), stop=(dc == nchunk - 1)),
                              inc=s_t if ct == nct - 1 else None)
            sq_guard[j] = mm_dep
        return mm_dep

    def ln_finish(stats_dep, T1, T2, SCR, width=2048, denom=2048.0,
                  xb=None, yb=None):
        nct = width // 512
        xb = xb or list(range(nct))
        yb = yb or list(range(4, 4 + nct))
        evs = []
        for ct in range(nct):
            evs.append(b.op("v", lambda e, ct=ct, T1=T1, bi=xb[ct]:
                            e.tensor_copy(T1[0:1, ct * 512:(ct + 1) * 512],
                                          pb[bi][0:1, :]),
                            deps=[stats_dep] + rows_guard, inc=s_v))
            evs.append(b.op("v", lambda e, ct=ct, T2=T2, bi=yb[ct]:
                            e.tensor_copy(T2[0:1, ct * 512:(ct + 1) * 512],
                                          pb[bi][0:1, :]),
                            deps=[stats_dep], inc=s_v))
        dep = evs[-1]
        bc_x = []
        bc_xx = []
        for ct in range(nct):
            b.wait("t", [dep])
            g1 = b.op("t", lambda e, ct=ct, T1=T1, bi=xb[ct]:
                      e.matmul(pb[bi][:], ones_s[0:1, :],
                               T1[0:1, ct * 512:(ct + 1) * 512],
                               start=True, stop=True),
                      inc=s_t)
            bc_x.append(g1)
            g2 = b.op("t", lambda e, ct=ct, T2=T2, bi=yb[ct]:
                      e.matmul(pb[bi][:], ones_s[0:1, :],
                               T2[0:1, ct * 512:(ct + 1) * 512],
                               start=True, stop=True),
                      inc=s_t)
            bc_xx.append(g2)
        s4 = None
        for ct in range(nct):
            s4 = b.op("s", lambda e, ct=ct, SCR=SCR, bi=xb[ct]:
                      e.activation(out=SCR[:, ct * 512:(ct + 1) * 512],
                                   in_=pb[bi][:], func=AF.Copy,
                                   scale=1.0 / denom),
                      deps=[bc_x[ct]], inc=s_a)
            banks.done(xb[ct], s4)
        s5 = b.op("s", lambda e, T1=T1, SCR=SCR, w=width:
                  e.activation(out=T1[:, 0:w], in_=SCR[:, 0:w],
                               func=AF.Square),
                  deps=[bc_x[-1], bc_xx[-1]], inc=s_a)
        s6 = None
        for ct in range(nct):
            s6 = b.op("v", lambda e, ct=ct, T1=T1, T2=T2, bi=yb[ct]:
                      e.scalar_tensor_tensor(
                          T2[:, ct * 512:(ct + 1) * 512], pb[bi][:],
                          1.0 / denom, T1[:, ct * 512:(ct + 1) * 512],
                          ALU.mult, ALU.subtract),
                      deps=[s5], inc=s_v)
            banks.done(yb[ct], s6)
        b.op("s", lambda e, T1=T1, T2=T2, w=width:
             e.activation(out=T1[:, 0:w], in_=T2[:, 0:w], func=AF.Ln,
                          bias=cval_s[:, 0:1]),
             deps=[s6, consts_dep])
        s8 = b.op("s", lambda e, T1=T1, T2=T2, w=width:
                  e.activation(out=T2[:, 0:w], in_=T1[:, 0:w], func=AF.Exp,
                               scale=-0.5), inc=s_a)
        s9 = b.op("v", lambda e, T1=T1, T2=T2, SCR=SCR, w=width:
                  e.tensor_mul(T1[:, 0:w], SCR[:, 0:w], T2[:, 0:w]),
                  deps=[s8], inc=s_v)
        rows_guard.clear()
        rows_guard.extend([s9])
        return T2[:], T1[:], s9

    stA = ExitStack()
    sqA = [sbuf(stA, f"sqA{i}", [128, 2048], BF16) for i in range(2)]
    T1A = sbuf(stA, "T1A", [128, 2048], F32)
    T2A = sbuf(stA, "T2A", [128, 2048], F32)
    SCRA = sbuf(stA, "SCRA", [128, 2048], F32)
    tmp_bf = sbuf(stA, "tmp_bf", [128, 2048], BF16)

    xb_cast = [None] * NDC
    for dc in range(NDC):
        xb_cast[dc] = b.op("g", lambda e, dc=dc:
                           e.dma_start(out=buf1[:, dc, :],
                                       in_=xT[dc * 128:(dc + 1) * 128, :]),
                           inc=s_gw, inck=16)

    a_stats = stats_pass(lambda dc: buf1[:, dc, :], sqA,
                         lambda dc: xb_cast[dc])
    rstdA, mrA, a_bcast = ln_finish(a_stats, T1A, T2A, SCRA)

    xn1_done = None
    for dc in range(NDC):
        b.op("v", lambda e, dc=dc: e.tensor_mul(tmp_bf[:], buf1[:, dc, :],
                                                rstdA),
             deps=[a_bcast])
        xn1_done = b.op("v", lambda e, dc=dc:
                        e.tensor_sub(buf1[:, dc, :], tmp_bf[:], mrA),
                        inc=s_v)

    if debug == "xn1":
        dump(lambda c: buf1[:, c, :], NDC, 2048, [xn1_done])

    stA.close()
    b.gate()

    stQ = ExitStack()
    q2_t = sbuf(stQ, "q2_t", [128, 8, 2048], BF16)
    kv_sb = sbuf(stQ, "kv_sb", [128, 8, 258], BF16)
    sc32_t = sbuf(stQ, "sc32", [128, 16384], BF16)
    k2T = sc32_t[:].rearrange("p (s q) -> p s q", s=16)
    q2s = sc32_t[:].rearrange("p (j q) -> p j q", j=8)

    stGB = ExitStack()
    gbuf = [sbuf(stGB, f"gbuf{i}", [128, 1024], F32) for i in range(2)]
    gamma_srcs = []
    for w_ap in (w1_t, w2_t):
        for hc in range(NHC):
            for half in range(2):
                gamma_srcs.append(w_ap[hc, :, half * 1024:(half + 1) * 1024])
    for g4 in range(4):
        for hc in range(0, NHC, 2):
            gamma_srcs.append(w3_t[g4, hc:hc + 2].rearrange("h p c -> p h c"))
    NG = len(gamma_srcs)
    g_state = {"dma": 0, "act": 0}
    g_dma_tgt = [None] * NG
    g_act_tgt = [None] * NG

    def pump_gamma_dma(n):
        while g_state["dma"] < min(g_state["act"] + 2, NG) and n > 0:
            k = g_state["dma"]
            dep = g_act_tgt[k - 2] if k >= 2 else None
            src_ap = gamma_srcs[k]
            dst_ap = gbuf[k % 2][:]
            if len(src_ap.shape) == 3:
                dst_ap = dst_ap.rearrange("p (h c) -> p h c", h=src_ap.shape[1])
            g_dma_tgt[k] = b.op("y", lambda e, k=k, d=dst_ap, s=src_ap:
                                e.dma_start(out=d, in_=s),
                                deps=[dep], inc=s_sl, inck=16)
            g_state["dma"] += 1
            n -= 1

    def pump_gamma_act(n):
        while n > 0 and g_state["act"] < NG:
            pump_gamma_dma(4)
            if g_state["act"] >= g_state["dma"]:
                break
            k = g_state["act"]
            g_act_tgt[k] = b.op("v", lambda e, k=k:
                                e.tensor_reduce(gacc[:, k:k + 1],
                                                gbuf[k % 2][:], AX.X, ALU.add,
                                                apply_absolute_value=True),
                                deps=[g_dma_tgt[k]], inc=s_v)
            g_state["act"] += 1
            n -= 1
        pump_gamma_dma(4)

    stB2 = ExitStack()
    vpair = [sbuf(stB2, f"vpair{i}", [128, 16, 258], BF16) for i in range(2)]
    wqt = [sbuf(stB2, f"wqt{i}", [128, 2048], BF16) for i in range(2)]
    wkt = [sbuf(stB2, f"wkt{i}", [128, 2048], BF16) for i in range(2)]
    wvt = [sbuf(stB2, f"wvt{i}", [128, 2048], BF16) for i in range(2)]
    q_sb = sbuf(stB2, "q_sb", [128, 2048], BF16)
    k_sb = sbuf(stB2, "k_sb", [128, 2048], BF16)
    v_f32 = [sbuf(stB2, f"vf32{i}", [128, 512], F32) for i in range(2)]
    e1 = sbuf(stB2, "e1", [128, 512], BF16)
    e2 = sbuf(stB2, "e2", [128, 512], F32)

    vp_ones = [b.op("v", lambda e, i=i: e.memset(vpair[i][:, :, 128:129], 1.0),
                    inc=s_v) for i in range(2)]

    w_guard = {"q": [None, None], "k": [None, None], "v": [None, None]}
    w_load = {"q": [None] * H, "k": [None] * H, "v": [None] * H}
    ring_i = {"i": 0}
    q_sb_guard = None
    k_sb_guard = None
    vf_guard = [None, None]
    e_guard = [None, None]
    kv_done = [None] * 8
    vpair_mm = [None, None]

    def load_head_w(ic):
        j = ic % 2
        for nm, wt, buf in (("q", wq_t, wqt), ("k", wk_t, wkt),
                            ("v", wv_t, wvt)):
            w_load[nm][ic] = b.op("g", lambda e, ic=ic, j=j, wt=wt, buf=buf:
                                  e.dma_start(out=buf[j][:], in_=wt[ic]),
                                  deps=[w_guard[nm][j]], inc=s_gw, inck=16)

    load_head_w(0)
    load_head_w(1)

    vpair_evac_done = None
    for ic in range(H):
        j = ic % 2
        pair = ic // 2
        r0 = (ic % 2) * 64
        vb = pair % 2

        for nm in ("q", "k"):
            wtl = wqt if nm == "q" else wkt
            dest = q_sb if nm == "q" else k_sb
            bias_col = ic if nm == "q" else H + ic
            evac_deps = []
            dep = None
            for ct in range(NCT):
                slot = ring_i["i"] % 3
                ring_i["i"] += 1
                bt = banks.use(slot)
                for dc in range(NDC):
                    dep = b.op("t", lambda e, ct=ct, dc=dc, j=j, bt=bt, wtl=wtl:
                               e.matmul(bt[:], wtl[j][:, dc * 128:(dc + 1) * 128],
                                        buf1[:, dc, ct * 512:(ct + 1) * 512],
                                        start=(dc == 0), stop=(dc == NDC - 1)),
                               deps=[w_load[nm][ic], xn1_done] if dc == 0 else None,
                               inc=s_t if dc == NDC - 1 else None)
                gdep = b.op("s", lambda e, ct=ct, bt=bt, dest=dest, bc=bias_col:
                            e.activation(out=dest[:, ct * 512:(ct + 1) * 512],
                                         in_=bt[:], func=AF.Identity,
                                         bias=bias_s[:, bc:bc + 1]),
                            deps=[dep, consts_dep,
                                  q_sb_guard if nm == "q" else k_sb_guard],
                            inc=s_a)
                banks.done(slot, gdep)
                evac_deps.append(gdep)
            w_guard[nm][j] = dep

            if nm == "q":
                for ct in range(NCT):
                    mslot = 3 + ct % 2
                    bt = banks.use(mslot)
                    md = b.op("t", lambda e, ct=ct, bt=bt:
                              e.matmul(bt[0:64, :], wmq_s[:, :],
                                       q_sb[:, ct * 512:(ct + 1) * 512],
                                       start=True, stop=True),
                              deps=[evac_deps[ct], wmaps_dep], inc=s_t)
                    q_sb_guard = md
                    d1 = b.op("v", lambda e, bt=bt:
                              e.tensor_scalar_min(e1[0:64, :], bt[0:64, :], 0.0),
                              deps=[md, e_guard[0]], inc=s_v)
                    d2 = b.op("s", lambda e:
                              e.activation(out=e2[0:64, :], in_=e1[0:64, :],
                                           func=AF.Exp),
                              deps=[d1, e_guard[1]], inc=s_a)
                    e_guard[0] = d2
                    d3 = b.op("v", lambda e, ct=ct, bt=bt, r0=r0, pair=pair:
                              e.scalar_tensor_tensor(
                                  q2_t[r0:r0 + 64, pair,
                                       ct * 512:(ct + 1) * 512],
                                  bt[0:64, :], 0.0, e2[0:64, :],
                                  ALU.max, ALU.add),
                              deps=[d2], inc=s_v)
                    e_guard[1] = d3
                    banks.done(mslot, d3)
                    pump_gamma_act(1)
            else:
                for pk in range(2):
                    mslot = 3 + pk
                    bt = banks.use(mslot)
                    md = None
                    for si in range(8):
                        sub = pk * 8 + si
                        md = b.op("t", lambda e, sub=sub, si=si, bt=bt:
                                  e.matmul(bt[:, si * 64:(si + 1) * 64],
                                           k_sb[:, sub * 128:(sub + 1) * 128],
                                           wmk_s[:, :], start=True, stop=True),
                                  deps=[evac_deps[sub // 4], wmaps_dep]
                                  if sub % 4 == 0 else None,
                                  inc=s_t if si == 7 else None)
                    k_sb_guard = md
                    d1 = b.op("v", lambda e, bt=bt:
                              e.tensor_scalar_min(e1[:, :], bt[:, :], 0.0),
                              deps=[md, e_guard[0]], inc=s_v)
                    d2 = b.op("s", lambda e:
                              e.activation(out=e2[:, :], in_=e1[:, :],
                                           func=AF.Exp),
                              deps=[d1, e_guard[1]], inc=s_a)
                    e_guard[0] = d2
                    d3 = b.op("v", lambda e, pk=pk, ic=ic, bt=bt:
                              e.scalar_tensor_tensor(
                                  k2T[:, pk * 8:(pk + 1) * 8,
                                      ic * 64:(ic + 1) * 64],
                                  bt[:].rearrange("p (s q) -> p s q", s=8),
                                  0.0,
                                  e2[:].rearrange("p (s q) -> p s q", s=8),
                                  ALU.max, ALU.add),
                              deps=[d2], inc=s_v)
                    e_guard[1] = d3
                    banks.done(mslot, d3)
                    pump_gamma_act(2)

        dep = None
        for ct in range(NCT):
            slot = ring_i["i"] % 3
            ring_i["i"] += 1
            bt = banks.use(slot)
            for dc in range(NDC):
                dep = b.op("t", lambda e, ct=ct, dc=dc, j=j, bt=bt:
                           e.matmul(bt[:], wvt[j][:, dc * 128:(dc + 1) * 128],
                                    buf1[:, dc, ct * 512:(ct + 1) * 512],
                                    start=(dc == 0), stop=(dc == NDC - 1)),
                           deps=[w_load["v"][ic]] if dc == 0 else None,
                           inc=s_t if dc == NDC - 1 else None)
            jv = ct % 2
            vfd = b.op("s", lambda e, ct=ct, bt=bt, ic=ic, jv=jv:
                       e.activation(out=v_f32[jv][:], in_=bt[:], func=AF.Identity,
                                    bias=bias_s[:, 2 * H + ic:2 * H + ic + 1]),
                       deps=[dep, vf_guard[jv]], inc=s_a)
            banks.done(slot, vfd)
            tslot = 5 + ct % 2
            bt2 = banks.use(tslot)
            td = None
            for qtr in range(4):
                td = b.op("t", lambda e, qtr=qtr, bt2=bt2, jv=jv:
                          e.transpose(bt2[:, qtr * 128:(qtr + 1) * 128],
                                      v_f32[jv][:, qtr * 128:(qtr + 1) * 128],
                                      ident_s[:]),
                          deps=[vfd, consts_dep] if qtr == 0 else None,
                          inc=s_t if qtr == 3 else None)
            vf_guard[jv] = td
            cofs = 0 if ic % 2 == 0 else 129
            ed = b.op("v", lambda e, ct=ct, bt2=bt2, vb=vb, cofs=cofs:
                      e.tensor_copy(
                          vpair[vb][:, ct * 4:(ct + 1) * 4, cofs:cofs + 128],
                          bt2[:].rearrange("p (s q) -> p s q", s=4)),
                      deps=[td, vp_ones[vb], vpair_mm[vb]], inc=s_v)
            banks.done(tslot, ed)
            pump_gamma_act(1)
            if ct == NCT - 1:
                vpair_evac_done = ed
        w_guard["v"][j] = dep

        pump_gamma_act(3)

        if ic % 2 == 1:
            bt = banks.use(7)
            kd = None
            for sub in range(16):
                kd = b.op("t", lambda e, sub=sub, pair=pair, bt=bt, vb=vb:
                          e.matmul(bt[:, 0:258],
                                   k2T[:, sub, pair * 128:(pair + 1) * 128],
                                   vpair[vb][:, sub, :],
                                   start=(sub == 0), stop=(sub == 15)),
                          deps=[e_guard[1], vpair_evac_done]
                          if sub == 0 else None,
                          inc=s_t if sub == 15 else None)
            vpair_mm[vb] = kd
            kv_done[pair] = b.op("v", lambda e, pair=pair, bt=bt:
                                 e.tensor_copy(kv_sb[:, pair, :], bt[:, 0:258]),
                                 deps=[kd], inc=s_v)
            banks.done(7, kv_done[pair])
            pump_gamma_act(2)

        if ic + 2 < H:
            load_head_w(ic + 2)

    pump_gamma_act(NG)

    if debug == "q2":
        dump(lambda c: q2_t[:, c, :], 8, 2048, [e_guard[1], kv_done[7]])
    if debug == "k2T":
        dump(lambda c: k2T[:, c, :], 16, 1024, [e_guard[1], kv_done[7]])
    if debug == "kv":
        dump(lambda c: kv_sb[:, c, :], 8, 258, [kv_done[7]])

    stB2.close()
    stGB.close()

    stD = ExitStack()
    r128 = sbuf(stD, "r128", [128, 2048], F32)
    rows_fD = sbuf(stD, "rows_fD", [128, 2048], F32)
    rows_bD = sbuf(stD, "rows_bD", [128, 2048], BF16)
    lnrow = rows_fD[0:1, :]
    recrow = rows_bD[0:1, :]
    b.gate(("t", "v", "s"))

    r128_guard = None
    recrow_guard = None
    attn_done = [None] * H
    for h in range(H):
        pair = h // 2
        r0 = (h % 2) * 64
        c0 = 0 if h % 2 == 0 else 129
        row_deps = []
        for ct in range(NCT):
            slot = ct % 2
            bt = banks.use(slot)
            md = b.op("t", lambda e, ct=ct, bt=bt, pair=pair, r0=r0:
                      e.matmul(bt[0:1, :],
                               kv_sb[r0:r0 + 64, pair, 128:129],
                               q2_t[r0:r0 + 64, pair, ct * 512:(ct + 1) * 512],
                               start=True, stop=True),
                      deps=[kv_done[pair]], inc=s_t)
            ld = b.op("s", lambda e, ct=ct, bt=bt:
                      e.activation(out=lnrow[:, ct * 512:(ct + 1) * 512],
                                   in_=bt[0:1, :], func=AF.Ln,
                                   bias=cval_s[0:1, 1:2]),
                      deps=[md], inc=s_a)
            banks.done(slot, ld)
            row_deps.append(ld)
        rd = b.op("s", lambda e: e.activation(out=recrow[:], in_=lnrow[:],
                                              func=AF.Exp, scale=-1.0),
                  deps=row_deps + [recrow_guard], inc=s_a)
        ev = None
        for ct in range(NCT):
            slot = 2 + ct % 2
            bt = banks.use(slot)
            g = b.op("t", lambda e, ct=ct, bt=bt:
                     e.matmul(bt[:], ones_b[0:1, :],
                              recrow[:, ct * 512:(ct + 1) * 512],
                              start=True, stop=True),
                     deps=[rd], inc=s_t)
            recrow_guard = g
            ev = b.op("v", lambda e, ct=ct, bt=bt:
                      e.tensor_copy(r128[:, ct * 512:(ct + 1) * 512], bt[:]),
                      deps=[g, r128_guard], inc=s_v)
            banks.done(slot, ev)
        qsd = b.op("v", lambda e, pair=pair, r0=r0:
                   e.tensor_mul(q2s[r0:r0 + 64, pair, :],
                                q2_t[r0:r0 + 64, pair, :],
                                r128[r0:r0 + 64, :]),
                   deps=[ev], inc=s_v)
        r128_guard = qsd
        for ct in range(NCT):
            slot = 4 + ct % 2
            bt = banks.use(slot)
            md = b.op("t", lambda e, ct=ct, bt=bt, pair=pair, r0=r0, c0=c0:
                      e.matmul(bt[:], kv_sb[r0:r0 + 64, pair, c0:c0 + 128],
                               q2s[r0:r0 + 64, pair, ct * 512:(ct + 1) * 512],
                               start=True, stop=True),
                      deps=[qsd], inc=s_t)
            ad = b.op("s", lambda e, ct=ct, h=h, bt=bt:
                      e.activation(out=buf1[:, h, ct * 512:(ct + 1) * 512],
                                   in_=bt[:], func=AF.Copy),
                      deps=[md], inc=s_a)
            banks.done(slot, ad)
        attn_done[h] = ad

    if debug == "attn":
        dump(lambda c: buf1[:, c, :], NDC, 2048, [attn_done[H - 1]])

    stD.close()
    stQ.close()

    stF = ExitStack()
    sqF = [sbuf(stF, f"sqF{i}", [128, 2048], BF16) for i in range(2)]
    ybfF = [sbuf(stF, f"ybfF{i}", [128, 2048], BF16) for i in range(2)]
    T1F = sbuf(stF, "T1F", [128, 2048], F32)
    T2F = sbuf(stF, "T2F", [128, 2048], F32)
    SCRF = sbuf(stF, "SCRF", [128, 2048], F32)
    xtF = sbuf(stF, "xtF", [128, 2048], F32)
    ytF = [sbuf(stF, f"ytF{i}", [128, 2048], F32) for i in range(2)]
    fs1 = [sbuf(stF, f"fs1_{i}", [128, 2048], F32) for i in range(2)]
    fs2 = [sbuf(stF, f"fs2_{i}", [128, 2048], F32) for i in range(2)]
    x2bf = [sbuf(stF, f"x2bf{i}", [128, 2048], BF16) for i in range(2)]
    xtF2 = sbuf(stF, "xtF2", [128, 2048], F32)
    stE2 = ExitStack()
    wot = [sbuf(stE2, f"wot{i}", [128, 2048], BF16) for i in range(2)]
    ytile = [sbuf(stE2, f"ytile{i}", [128, 2048], F32) for i in range(2)]
    ytile_guard = [None, None]
    ybf_guard = [None, None]
    sqg = [None, None]
    b.gate(("g", "s", "v"))
    wot_guard = [None, None]
    wot_load = [None] * H
    y_done = [None] * NDC

    def load_wo(ic):
        jj = ic % 2
        wot_load[ic] = b.op("g", lambda e, ic=ic, jj=jj:
                            e.dma_start(out=wot[jj][:], in_=wo_t[ic]),
                            deps=[wot_guard[jj]], inc=s_gw, inck=16)

    load_wo(0)
    load_wo(1)
    for ic in range(H):
        jj = ic % 2
        dep = None
        for ct in range(NCT):
            slot = ring_i["i"] % 3
            ring_i["i"] += 1
            bt = banks.use(slot)
            for dc in range(NDC):
                dep = b.op("t", lambda e, ct=ct, dc=dc, jj=jj, bt=bt:
                           e.matmul(bt[:], wot[jj][:, dc * 128:(dc + 1) * 128],
                                    buf1[:, dc, ct * 512:(ct + 1) * 512],
                                    start=(dc == 0), stop=(dc == NDC - 1)),
                           deps=[wot_load[ic], attn_done[dc]]
                           if dc == 0 else None,
                           inc=s_t if dc == NDC - 1 else None)
            yd = b.op("v", lambda e, ct=ct, ic=ic, jj=jj, bt=bt:
                      e.tensor_scalar(ytile[jj][:, ct * 512:(ct + 1) * 512],
                                      bt[:],
                                      bias_s[:, 3 * H + ic:3 * H + ic + 1],
                                      None, ALU.add),
                      deps=[dep, ytile_guard[jj]], inc=s_v)
            banks.done(slot, yd)
        y_done[ic] = b.op("g", lambda e, ic=ic, jj=jj:
                          e.dma_start(out=y_scr[ic], in_=ytile[jj][:]),
                          deps=[yd], inc=s_st, inck=16)
        ycast = b.op("v", lambda e, jj=jj:
                     e.tensor_copy(ybfF[jj][:], ytile[jj][:]),
                     deps=[yd, ybf_guard[jj]], inc=s_v)
        ytile_guard[jj] = [y_done[ic], ycast]
        ysq = b.op("s", lambda e, jj=jj:
                   e.activation(out=sqF[jj][:], in_=ybfF[jj][:],
                                func=AF.Square),
                   deps=[ycast, sqg[jj]], inc=s_a)
        st_dep = None
        for ct in range(NCT):
            bt = banks.use(3 + ct) if ic == 0 else pb[3 + ct]
            st_dep = b.op("t", lambda e, ic=ic, ct=ct, jj=jj, bt=bt:
                          e.matmul(bt[0:1, :], ones_b[:, 0:1],
                                   ybfF[jj][:, ct * 512:(ct + 1) * 512],
                                   start=(ic == 0), stop=(ic == H - 1)),
                          deps=[ysq] if ct == 0 else None,
                          inc=s_t if ct == NCT - 1 else None)
        for ct in range(1):
            bt = banks.use(7) if ic == 0 else pb[7]
            st_dep = b.op("t", lambda e, ic=ic, jj=jj, bt=bt:
                          e.matmul(bt[0:1, :], ones_b[:, 0:1],
                                   sqF[jj][:, 0:512],
                                   start=(ic == 0), stop=(ic == H - 1)),
                          inc=s_t)
        ybf_guard[jj] = st_dep
        sqg[jj] = st_dep
        wot_guard[jj] = dep
        if ic + 2 < H:
            load_wo(ic + 2)
    pass0_done = st_dep

    stE2.close()

    b.gate(("g",))
    yt_guard = [None, None]
    f_stats = None
    p2sq = [None, None]
    for ic in range(NDC):
        j = ic % 2
        ld = b.op("y", lambda e, ic=ic, j=j:
                  e.dma_start(out=ytF[j][:], in_=y_scr[ic]),
                  deps=[y_done[ic], yt_guard[j]], inc=s_sl, inck=16)
        ycast = b.op("v", lambda e, j=j:
                     e.tensor_copy(ybfF[j][:, 512:2048],
                                   ytF[j][:, 512:2048]),
                     deps=[ld, p2sq[j], pass0_done], inc=s_v)
        yt_guard[j] = ycast
        ysq = b.op("s", lambda e, j=j:
                   e.activation(out=sqF[j][:, 512:2048],
                                in_=ybfF[j][:, 512:2048], func=AF.Square),
                   deps=[ycast], inc=s_a)
        for ct in range(3):
            bt = banks.use(ct) if ic == 0 else pb[ct]
            f_stats = b.op("t", lambda e, ic=ic, ct=ct, j=j, bt=bt:
                           e.matmul(bt[0:1, :], ones_b[:, 0:1],
                                    sqF[j][:, 512 + ct * 512:
                                            1024 + ct * 512],
                                    start=(ic == 0), stop=(ic == NDC - 1)),
                           deps=[ysq] if ct == 0 else None,
                           inc=s_t if ct == 2 else None)
        p2sq[j] = f_stats
        rstdF, mrF, f_bcast = ln_finish(f_stats, T1F, T2F, SCRF,
                                    xb=[3, 4, 5, 6], yb=[7, 0, 1, 2])
    if debug == "lnf":
        b.op("g", lambda e: e.dma_start(out=dbg[0], in_=T2F[:]),
             deps=[f_bcast], inc=s_st, inck=16)
        b.op("g", lambda e: e.dma_start(out=dbg[1], in_=T1F[:]),
             deps=[f_bcast], inc=s_st, inck=16)

    x2_store = [None] * NDC
    xt_g = [None, None]
    fs_g = [None, None]
    cb_g = [None, None]
    sq_g = [None, None]
    x3_mm = None
    for dc in range(NDC):
        j = dc % 2
        xts = [xtF, xtF2]
        ld = b.op("y", lambda e, dc=dc, j=j, xts=xts:
                  e.dma_start(out=xts[j][:], in_=xT[dc * 128:(dc + 1) * 128, :]),
                  deps=[xt_g[j]], inc=s_sl, inck=16)
        yld = b.op("y", lambda e, dc=dc, j=j:
                   e.dma_start(out=ytF[j][:], in_=y_scr[dc]),
                   deps=[yt_guard[j]], inc=s_sl, inck=16)
        b.op("v", lambda e, j=j: e.tensor_mul(fs1[j][:], ytF[j][:], rstdF),
             deps=[f_bcast, fs_g[j], yld])
        sub_d = b.op("v", lambda e, j=j: e.tensor_sub(fs2[j][:], fs1[j][:], mrF),
                     inc=s_v)
        if debug == "f3y" and dc == 4:
            b.op("g", lambda e, j=j: e.dma_start(out=dbg[0], in_=ytF[j][:]),
                 deps=[sub_d], inc=s_st, inck=16)
            b.op("g", lambda e, j=j: e.dma_start(out=dbg[1], in_=fs1[j][:]),
                 deps=[sub_d], inc=s_st, inck=16)
            b.op("g", lambda e, j=j: e.dma_start(out=dbg[2], in_=fs2[j][:]),
                 deps=[sub_d], inc=s_st, inck=16)
            b.op("g", lambda e: e.dma_start(out=dbg[3], in_=T2F[:]),
                 deps=[sub_d], inc=s_st, inck=16)
            b.wait("v", (s_st, s_st.n))
        x2d = b.op("v", lambda e, j=j, xts=xts:
                   e.tensor_add(fs1[j][:], fs2[j][:], xts[j][:]),
                   deps=[ld, sub_d], inc=s_v)
        yt_guard[j] = x2d
        xt_g[j] = x2d
        x2_store[dc] = b.op("g", lambda e, dc=dc, j=j:
                            e.dma_start(out=x2_scr[dc], in_=fs1[j][:]),
                            deps=[x2d], inc=s_st, inck=16)
        fs_g[j] = x2_store[dc]
        cbd = b.op("v", lambda e, j=j: e.tensor_copy(x2bf[j][:], fs1[j][:]),
                   deps=[cb_g[j], x2d], inc=s_v)
        sqd = b.op("s", lambda e, j=j:
                   e.activation(out=sqF[j][:], in_=x2bf[j][:], func=AF.Square),
                   deps=[cbd, sq_g[j]], inc=s_a)
        for ct in range(NCT):
            bt = banks.use(ct) if dc == 0 else pb[ct]
            b.op("t", lambda e, dc=dc, ct=ct, j=j, bt=bt:
                 e.matmul(bt[0:1, :], ones_b[:, 0:1],
                          x2bf[j][:, ct * 512:(ct + 1) * 512],
                          start=(dc == 0), stop=(dc == NDC - 1)),
                 deps=[sqd] if ct == 0 else None)
        for ct in range(NCT):
            bt = banks.use(4 + ct) if dc == 0 else pb[4 + ct]
            x3_mm = b.op("t", lambda e, dc=dc, ct=ct, j=j, bt=bt:
                         e.matmul(bt[0:1, :], ones_b[:, 0:1],
                                  sqF[j][:, ct * 512:(ct + 1) * 512],
                                  start=(dc == 0), stop=(dc == NDC - 1)),
                         inc=s_t if ct == NCT - 1 else None)
        cb_g[j] = x3_mm
        sq_g[j] = x3_mm
        b.gate(("s",))
    rstd3, mr3, x3_bcast = ln_finish(x3_mm, T1F, T2F, SCRF)

    xn2_done = None
    for dc in range(NDC):
        j = dc % 2
        xts = [xtF, xtF2]
        ld = b.op("y", lambda e, dc=dc, j=j, xts=xts:
                  e.dma_start(out=xts[j][:], in_=x2_scr[dc]),
                  deps=[xt_g[j], x2_store[dc]], inc=s_sl, inck=16)
        b.op("v", lambda e, j=j, xts=xts: e.tensor_mul(fs2[j][:], xts[j][:],
                                                       rstd3),
             deps=[ld, x3_bcast, fs_g[j]])
        xn2_done = b.op("v", lambda e, dc=dc, j=j:
                        e.tensor_sub(buf1[:, dc, :], fs2[j][:], mr3),
                        inc=s_v)
        xt_g[j] = xn2_done
        fs_g[j] = xn2_done

    if debug == "xn2":
        dump(lambda c: buf1[:, c, :], NDC, 2048, [xn2_done])

    stF.close()

    stT = ExitStack()
    gcol = sbuf(stT, "gcol", [128, 3], F32)
    throwt = sbuf(stT, "throwt", [128, 8], F32)
    throw = throwt[0:1, :]
    gamma_last = (s_a, g_act_tgt[NG - 1][1])
    t2 = None
    for w in range(3):
        rdep = b.op("v", lambda e, w=w:
                    e.tensor_reduce(gcol[:, w:w + 1],
                                    gacc[:, w * 88:(w + 1) * 88],
                                    AX.X, ALU.add),
                    deps=[gamma_last], inc=s_v)
        bt = banks.use(w % 2)
        md = b.op("t", lambda e, w=w, bt=bt:
                  e.matmul(bt[0:1, 0:1], ones_s[:, 0:1], gcol[:, w:w + 1],
                           start=True, stop=True),
                  deps=[rdep], inc=s_t)
        t1 = b.op("s", lambda e, w=w, bt=bt:
                  e.activation(out=throw[:, 2 * w:2 * w + 1], in_=bt[0:1, 0:1],
                               func=AF.Identity, scale=0.5 / NW,
                               bias=cval_s[0:1, 2:3]),
                  deps=[md], inc=s_a)
        banks.done(w % 2, t1)
        t2 = b.op("s", lambda e, w=w:
                  e.activation(out=throw[:, 2 * w + 1:2 * w + 2],
                               in_=throw[:, 2 * w:2 * w + 1],
                               func=AF.Copy, scale=-1.0),
                  inc=s_a)
    bt = banks.use(2)
    md = b.op("t", lambda e, bt=bt:
              e.matmul(bt[:, 0:6], ones_s[0:1, :], throw[:, 0:6],
                       start=True, stop=True),
              deps=[t2], inc=s_t)
    thr_dep = b.op("v", lambda e, bt=bt: e.tensor_copy(thrs[:], bt[:, 0:6]),
                   deps=[md], inc=s_v)
    banks.done(2, thr_dep)
    stT.close()

    stN = ExitStack()
    h_b = sbuf(stN, "h_b", [128, NHC, 512], BF16)
    f_b = sbuf(stN, "f_b", [128, NDC, 512], F32)
    w1f = sbuf(stN, "w1f", [128, 2048], F32)
    w2f = sbuf(stN, "w2f", [128, 2048], F32)
    qw1 = [sbuf(stN, f"qw1_{i}", [128, 2048], BF16) for i in range(2)]
    qw2 = [sbuf(stN, f"qw2_{i}", [128, 2048], BF16) for i in range(2)]
    w3f = [sbuf(stN, f"w3f{i}", [128, 1024], F32) for i in range(2)]
    qw3g = [sbuf(stN, f"qw3g{i}", [128, 2048], BF16) for i in range(2)]
    qw3g_dep = [None, None]
    qw3g_guard = [None, None]
    sgt = [sbuf(stN, "sgt0", [128, 512], F32)] * 2
    x2t = [sbuf(stN, "x2t0", [128, 512], F32)] * 2
    of1 = sbuf(stN, "of1", [128, 512], F32)
    of2 = sbuf(stN, "of2", [128, 512], F32)
    sqN = sbuf(stN, "sqN", [128, 2048], BF16)
    qtb = sqN
    fbf = [sqN[:, 1024 + i * 512:1536 + i * 512] for i in range(2)]
    T1N = sbuf(stN, "T1N", [128, 512], F32)
    T2N = sbuf(stN, "T2N", [128, 512], F32)
    SCRN = of2
    b.gate()

    w1f_guard = None
    w2f_guard = None
    qw_guard = [[None, None], [None, None]]
    w3f_guard = [None, None]
    qw3_guard = [None, None]
    sgt_guard = [None, None]
    x2t_guard = [None, None]
    h_guard = [None] * NHC
    f_guard = [None] * NDC
    qtb_state = {"g": None}
    out_stores = []
    qw1_store = [None] * NHC
    qw2_store = [None] * NHC
    qw3_store = [[None] * NHC for _ in range(4)]

    def quantize(load_dep, wf_ap, qw_ap, widx, qg):
        d1 = b.op("v", lambda e, wf=wf_ap: e.tensor_scalar(
            qtb[:, 0:wf.shape[1]], wf,
            thrs[:, 2 * widx + 1:2 * widx + 2], None, ALU.is_le),
            deps=[load_dep, thr_dep, qtb_state["g"]], inc=s_v)
        d2 = b.op("v", lambda e, wf=wf_ap, qw=qw_ap: e.scalar_tensor_tensor(
            qw, wf, thrs[:, 2 * widx:2 * widx + 1], qtb[:, 0:wf.shape[1]],
            ALU.is_ge, ALU.subtract), deps=[qg], inc=s_v)
        qtb_state["g"] = d2
        return d2

    for blk in range(NB):
        c0 = blk * 512
        for hc in range(NHC):
            j = hc % 2
            if blk == 0:
                l1 = b.op("y", lambda e, hc=hc:
                          e.dma_start(out=w1f[:], in_=w1_t[hc]),
                          deps=[w1f_guard], inc=s_sl, inck=16)
                q1d = quantize(l1, w1f[:], qw1[j][:], 0, qw_guard[0][j])
                w1f_guard = q1d
                qw1_store[hc] = b.op("g", lambda e, hc=hc, j=j:
                     e.dma_start(out=qw1_scr[hc], in_=qw1[j][:]),
                     deps=[q1d], inc=s_st, inck=16)
                qw_guard[0][j] = [qw_guard[0][j], qw1_store[hc]]
                l2 = b.op("y", lambda e, hc=hc:
                          e.dma_start(out=w2f[:], in_=w2_t[hc]),
                          deps=[w2f_guard], inc=s_sl, inck=16)
                q2d = quantize(l2, w2f[:], qw2[j][:], 1, qw_guard[1][j])
                w2f_guard = q2d
                qw2_store[hc] = b.op("g", lambda e, hc=hc, j=j:
                     e.dma_start(out=qw2_scr[hc], in_=qw2[j][:]),
                     deps=[q2d], inc=s_st, inck=16)
                qw_guard[1][j] = [qw_guard[1][j], qw2_store[hc]]
                qw1_dep, qw2_dep = q1d, q2d
            else:
                qw1_dep = b.op("y", lambda e, hc=hc, j=j:
                               e.dma_start(out=qw1[j][:], in_=qw1_scr[hc]),
                               deps=[qw_guard[0][j], qw1_store[hc]],
                               inc=s_sl, inck=16)
                qw2_dep = b.op("y", lambda e, hc=hc, j=j:
                               e.dma_start(out=qw2[j][:], in_=qw2_scr[hc]),
                               deps=[qw_guard[1][j], qw2_store[hc]],
                               inc=s_sl, inck=16)
            sA = (hc % 3) * 2
            sB = sA + 1
            btA = banks.use(sA)
            btB = banks.use(sB)
            dA = dB = None
            for dc in range(NDC):
                dA = b.op("t", lambda e, dc=dc, j=j, btA=btA, c0=c0:
                          e.matmul(btA[:], qw1[j][:, dc * 128:(dc + 1) * 128],
                                   buf1[:, dc, c0:c0 + 512],
                                   start=(dc == 0), stop=(dc == NDC - 1)),
                          deps=[qw1_dep, xn2_done] if dc == 0 else None,
                          inc=s_t if dc == NDC - 1 else None)
            for dc in range(NDC):
                dB = b.op("t", lambda e, dc=dc, j=j, btB=btB, c0=c0:
                          e.matmul(btB[:], qw2[j][:, dc * 128:(dc + 1) * 128],
                                   buf1[:, dc, c0:c0 + 512],
                                   start=(dc == 0), stop=(dc == NDC - 1)),
                          deps=[qw2_dep] if dc == 0 else None,
                          inc=s_t if dc == NDC - 1 else None)
            if blk == 0:
                qw_guard[0][j] = [dA, qw1_store[hc]]
                qw_guard[1][j] = [dB, qw2_store[hc]]
            else:
                qw_guard[0][j] = dA
                qw_guard[1][j] = dB
            sg = b.op("s", lambda e, j=j, btA=btA:
                      e.activation(out=sgt[j][:], in_=btA[:], func=AF.Sigmoid),
                      deps=[dA, sgt_guard[0]], inc=s_a)
            banks.done(sA, sg)
            hd = b.op("v", lambda e, hc=hc, j=j, btB=btB:
                      e.tensor_mul(h_b[:, hc, :], sgt[j][:], btB[:]),
                      deps=[sg, dB, h_guard[hc]], inc=s_v)
            sgt_guard[0] = hd
            banks.done(sB, hd)
            h_guard[hc] = hd

        if debug == "h_b" and blk == 1:
            dump(lambda c: h_b[:, c, :], NHC, 512, [h_guard[NHC - 1]])

        for g4 in range(4):
            gb = (g4 % 2) * 4
            f_mm = None
            for hc in range(NHC):
                jg = (hc // 4) % 2
                if blk == 0:
                    if hc % 2 == 0:
                        jw = (hc // 2) % 2
                        l3 = b.op("y", lambda e, g4=g4, hc=hc, jw=jw:
                                  e.dma_start(
                                      out=w3f[jw][:].rearrange(
                                          "p (h c) -> p h c", h=2),
                                      in_=w3_t[g4, hc:hc + 2].rearrange(
                                          "h p c -> p h c")),
                                  deps=[w3f_guard[jw]], inc=s_sl, inck=16)
                        dst = qw3g[jg][:, (hc % 4) * 512:(hc % 4) * 512 + 1024]
                        q3d = quantize(l3, w3f[jw][:], dst, 2,
                                       qw3g_guard[jg] if hc % 4 == 0 else None)
                        w3f_guard[jw] = q3d
                        qw3_dep = q3d
                    if hc % 4 == 2:
                        qst = b.op("g", lambda e, g4=g4, hc=hc, jg=jg:
                                   e.dma_start(out=qw3_scr[g4, hc // 4],
                                               in_=qw3g[jg][:]),
                                   deps=[q3d], inc=s_st, inck=16)
                        for i in range(4):
                            qw3_store[g4][(hc // 4) * 4 + i] = qst
                else:
                    if hc % 4 == 0:
                        qw3g_dep[jg] = b.op(
                            "y", lambda e, g4=g4, hc=hc, jg=jg:
                            e.dma_start(out=qw3g[jg][:],
                                        in_=qw3_scr[g4, hc // 4]),
                            deps=[qw3g_guard[jg], qw3_store[g4][hc]],
                            inc=s_sl, inck=16)
                    qw3_dep = qw3g_dep[jg]
                cof = (hc % 4) * 512
                for i4 in range(4):
                    bt = banks.use(gb + i4) if hc == 0 else pb[gb + i4]
                    f_mm = b.op("t", lambda e, hc=hc, i4=i4, jg=jg, cof=cof,
                                bt=bt:
                                e.matmul(bt[:],
                                         qw3g[jg][:, cof + i4 * 128:
                                                   cof + (i4 + 1) * 128],
                                         h_b[:, hc, :],
                                         start=(hc == 0), stop=(hc == NHC - 1)),
                                deps=[qw3_dep, h_guard[hc]]
                                if i4 == 0 else None,
                                inc=s_t if i4 == 3 else None)
                if hc % 4 == 3 or hc == NHC - 1:
                    if blk == 0:
                        qw3g_guard[jg] = [f_mm, qw3_store[g4][hc]]
                    else:
                        qw3g_guard[jg] = f_mm
                h_guard[hc] = f_mm
            for i4 in range(4):
                ic = g4 * 4 + i4
                fd = b.op("s", lambda e, ic=ic, i4=i4, gb=gb:
                          e.activation(out=f_b[:, ic, :], in_=pb[gb + i4][:],
                                       func=AF.Copy),
                          deps=[f_mm, f_guard[ic]], inc=s_a)
                banks.done(gb + i4, fd)
                f_guard[ic] = fd

        fsq_guard = [None, None]
        st_mm = None
        for ic in range(NDC):
            j = ic % 2
            fcd = b.op("v", lambda e, ic=ic, j=j:
                       e.tensor_copy(fbf[j], f_b[:, ic, :]),
                       deps=[f_guard[ic], fsq_guard[j]], inc=s_v)
            sqd = b.op("s", lambda e, j=j:
                       e.activation(out=sqN[:, j * 512:(j + 1) * 512],
                                    in_=fbf[j], func=AF.Square),
                       deps=[fcd], inc=s_a)
            bt0 = banks.use(6) if ic == 0 else pb[6]
            b.op("t", lambda e, j=j, ic=ic, bt0=bt0:
                 e.matmul(bt0[0:1, :], ones_b[:, 0:1], fbf[j],
                          start=(ic == 0), stop=(ic == NDC - 1)),
                 deps=[sqd] if True else None)
            bt1 = banks.use(7) if ic == 0 else pb[7]
            st_mm = b.op("t", lambda e, ic=ic, j=j, bt1=bt1:
                         e.matmul(bt1[0:1, :], ones_b[:, 0:1],
                                  sqN[:, j * 512:(j + 1) * 512],
                                  start=(ic == 0), stop=(ic == NDC - 1)),
                         inc=s_t)
            fsq_guard[j] = st_mm
        rstdN, mrN, b4 = ln_finish(st_mm, T1N, T2N, SCRN, width=512,
                                   xb=[6], yb=[7])

        for ic in range(NDC):
            ld = b.op("g", lambda e, ic=ic, c0=c0:
                      e.dma_start(out=x2t[0][:],
                                  in_=x2_scr[ic, :, c0:c0 + 512]),
                      deps=[x2_store[ic]], inc=s_st, inck=16)
            b.op("g", lambda e, ic=ic: e.tensor_mul(of1[:], f_b[:, ic, :],
                                                    rstdN),
                 deps=[b4, out_stores[-1] if out_stores else None])
            b.op("g", lambda e: e.tensor_sub(of2[:], of1[:], mrN))
            od = b.op("g", lambda e: e.tensor_add(of1[:], of2[:], x2t[0][:]),
                      deps=[ld], inc=s_p)
            f_guard[ic] = od
            sd = b.op("g", lambda e, ic=ic, c0=c0:
                      e.dma_start(out=out_t[ic, :, c0:c0 + 512], in_=of1[:]),
                      inc=s_st, inck=16)
            out_stores.append(sd)

    if debug == "qw1":
        for hc in range(16):
            b.op("g", lambda e, hc=hc:
                 e.dma_start(out=dbg[hc], in_=qw1_scr[hc]),
                 deps=[(s_st, s_st.n)], inc=s_st, inck=16)

    b.wait("g", (s_st, s_st.n))
    stN.close()

    with nc.Block() as block:
        @block.tensor
        def _(t):
            for f in b.ops["t"]:
                f(t)

        @block.vector
        def _(v):
            for f in b.ops["v"]:
                f(v)

        @block.scalar
        def _(s):
            for f in b.ops["s"]:
                f(s)

        @block.gpsimd
        def _(g):
            for f in b.ops["g"]:
                f(g)

        @block.sync
        def _(y):
            for f in b.ops["y"]:
                f(y)

    nc._dbg_sems = {"s_t": s_t.n, "s_v": s_v.n, "s_a": s_a.n,
                    "s_gw": s_gw.n, "s_sl": s_sl.n, "s_st": s_st.n}
    top.close()
    return nc


def _tile_proj(w):
    O, I = w.shape
    noc = O // 128
    ndc = I // 128
    return np.ascontiguousarray(
        w.reshape(noc, 128, ndc, 128).transpose(0, 3, 2, 1).reshape(noc, 128, I))


def _tile_w3(w3):
    w3T = np.ascontiguousarray(w3.T)
    return np.ascontiguousarray(
        w3T.reshape(44, 128, 4, 512).transpose(2, 0, 1, 3))


_nc_cache = {}


def _get_nc(debug=None):
    if debug not in _nc_cache:
        _nc_cache[debug] = build_nc(debug=debug)
    return _nc_cache[debug]


def make_inputs(ins):
    f32 = np.float32
    shared = {
        "wq_t": _tile_proj(np.asarray(ins["L0_attn_wq"], f32)),
        "wk_t": _tile_proj(np.asarray(ins["L0_attn_wk"], f32)),
        "wv_t": _tile_proj(np.asarray(ins["L0_attn_wv"], f32)),
        "wo_t": _tile_proj(np.asarray(ins["L0_attn_wo"], f32)),
        "wmq_t": np.ascontiguousarray(np.asarray(ins["L0_attn_wmap_q"], f32).T),
        "wmk_t": np.ascontiguousarray(np.asarray(ins["L0_attn_wmap_k"], f32).T),
        "bias_t": np.ascontiguousarray(np.concatenate([
            np.asarray(ins[k], f32).reshape(16, 128).T
            for k in ("L0_attn_bq", "L0_attn_bk", "L0_attn_bv", "L0_attn_bo")],
            axis=1)),
        "w1_t": _tile_proj(np.asarray(ins["L0_ffn_w1"], f32)),
        "w2_t": _tile_proj(np.asarray(ins["L0_ffn_w2"], f32)),
        "w3_t": _tile_w3(np.asarray(ins["L0_ffn_w3"], f32)),
        "ones_bf": np.ones((128, 128), ml_dtypes.bfloat16),
        "ones_f": np.ones((128, 128), np.float32),
        "ident_f": np.eye(128, dtype=np.float32),
        "cvals": np.broadcast_to(
            np.array([LN_EPS, ATTN_EPS, FFN_HALF_EPS, 1.0 / 2048, 0, 0, 0, 0],
                     np.float32), (128, 8)).copy(),
    }
    x = np.asarray(ins["L0_attn_hidden_states"], np.float32)
    in_maps = []
    for c in range(8):
        m = dict(shared)
        m["xT"] = np.ascontiguousarray(x[c].T)
        in_maps.append(m)
    return in_maps


def kernel(**ins):
    nc = _get_nc(DEBUG)
    in_maps = make_inputs(ins)
    res = run_bass_kernel_spmd(nc, in_maps, core_ids=list(range(8)))
    out = np.empty((8, 2048, 2048), np.float32)
    for c in range(8):
        o = np.asarray(res.results[c]["out_t"], np.float32).reshape(2048, 2048)
        out[c] = o.T
    if DEBUG is not None:
        kernel.last_results = res
    return out
